# version 02
# baseline (speedup 1.0000x reference)
import contextlib
import numpy as np
import ml_dtypes
import concourse.bass as bass
import concourse.mybir as mybir
from concourse.bass_utils import run_bass_kernel_spmd

F32 = mybir.dt.float32
BF16 = mybir.dt.bfloat16
AF = mybir.ActivationFunctionType
ALU = mybir.AluOpType
BF16NP = ml_dtypes.bfloat16

ENGINES = ("tensor", "vector", "scalar", "gpsimd", "sync")


class _Op:
    __slots__ = ("eng", "emit", "is_dma", "idx", "sem_key", "need_inc", "deps")

    def __init__(self, eng, emit, is_dma, idx, sem_key):
        self.eng = eng
        self.emit = emit
        self.is_dma = is_dma
        self.idx = idx
        self.sem_key = sem_key
        self.need_inc = False
        self.deps = {}


class Sched:

    def __init__(self, nc):
        self.nc = nc
        self.ops = {e: [] for e in ENGINES}
        self.dma_counts = {}
        self.last_write = {}
        self.readers = {}

    def _dep(self, op, src):
        if src is None:
            return
        if src.is_dma:
            k = ("dma", src.sem_key)
            val = self.dma_counts[src.sem_key]
            if op.deps.get(k, 0) < val:
                op.deps[k] = val
            return
        if src.eng == op.eng and src.eng == "tensor":
            return
        k = ("op", src.eng)
        if op.deps.get(k, -1) < src.idx:
            op.deps[k] = src.idx
        src.need_inc = True

    def op(self, eng, emit, reads=(), writes=(), dma=False, stream="d"):
        sem_key = (eng, stream) if dma else None
        o = _Op(eng, emit, dma, len(self.ops[eng]), sem_key)
        if not dma and eng != "tensor":
            for prev in reversed(self.ops[eng]):
                if not prev.is_dma:
                    o.deps[("op", eng)] = prev.idx
                    prev.need_inc = True
                    break
        for k in reads:
            self._dep(o, self.last_write.get(k))
            self.readers.setdefault(k, []).append(o)
        for k in writes:
            self._dep(o, self.last_write.get(k))
            for r in self.readers.get(k, ()):
                if r is not o:
                    self._dep(o, r)
            self.last_write[k] = o
            self.readers[k] = []
        if dma:
            self.dma_counts[sem_key] = self.dma_counts.get(sem_key, 0) + 1
        self.ops[eng].append(o)
        return o

    def dma(self, eng, out, in_, reads=(), writes=(), stream="d", **kw):
        return self.op(eng, lambda E: E.dma_start(out=out, in_=in_, **kw),
                       reads=reads, writes=writes, dma=True, stream=stream)

    def dmat(self, eng, out, in_, reads=(), writes=(), stream="t", note=""):
        return self.op(eng, lambda E: E.dma_start_transpose(out, in_).annotate(note),
                       reads=reads, writes=writes, dma=True, stream=stream)

    def emit(self, block, final_wait_eng="sync"):
        nc = self.nc
        es = contextlib.ExitStack()
        prog_sem, dma_sem = {}, {}
        for e in ENGINES:
            if any(o.need_inc for o in self.ops[e]):
                prog_sem[e] = es.enter_context(nc.semaphore(f"P_{e}"))
        for sk in self.dma_counts:
            dma_sem[sk] = es.enter_context(nc.semaphore(f"D_{sk[0]}_{sk[1]}"))
        inc_count = {}
        for e in ENGINES:
            c, lst = 0, []
            for o in self.ops[e]:
                if o.need_inc:
                    c += 1
                lst.append(c)
            inc_count[e] = lst
        ungated = set(self.dma_counts)
        for e in ENGINES:
            for o in self.ops[e]:
                if o.is_dma and o.deps:
                    ungated.discard(o.sem_key)

        def run_engine(e, engine):
            for o in self.ops[e]:
                for dk, tgt in sorted(o.deps.items(), key=str):
                    kind = dk[0]
                    if kind == "op":
                        val, sem = inc_count[dk[1]][tgt], prog_sem[dk[1]]
                    else:
                        cnt = self.dma_counts[dk[1]] if dk[1] in ungated else tgt
                        val, sem = 16 * cnt, dma_sem[dk[1]]
                    engine.wait_ge(sem, val)
                inst = o.emit(engine)
                if o.is_dma:
                    inst.then_inc(dma_sem[o.sem_key], 16)
                elif o.need_inc:
                    inst.then_inc(prog_sem[e], 1)
            if e == final_wait_eng:
                for sk, cnt in self.dma_counts.items():
                    engine.wait_ge(dma_sem[sk], 16 * cnt)

        with es:
            for e in ENGINES:
                if self.ops[e] or e == final_wait_eng:
                    getattr(block, e)(lambda engine, e=e: run_engine(e, engine))


def pk(prefix, col0, col1):
    return [f"{prefix}:{b}" for b in range(col0 // 512, (col1 + 511) // 512)]


B, T, D = 2, 8192, 1024
TPC = 2048
NCORES = 8
EXT = 768
BUF = 2944
REAL0, REAL1 = EXT, EXT + TPC
D_INNER, D_STATE, NSSM = 2048, 64, 5
OFFS = [REAL0 - (NSSM - l) * 131 for l in range(NSSM)]
NCHUNK = [(REAL1 - off + 127) // 128 for off in OFFS]
LN_EPS = 1e-5
PHASES = {"proj", "conv", "scan", "outproj", "ln"}


def ssm_groups(ell):
    n = NCHUNK[ell]
    out = []
    c = 0
    while c < n:
        g = min(4, n - c)
        out.append((c, g))
        c += g
    return out


def build_launch_A():
    nc = bass.Bass("TRN2", target_bir_lowering=False, debug=False)
    ap = {}

    def inp(name, shape):
        ap[name] = nc.dram_tensor(name, shape, F32, kind="ExternalInput").ap()

    inp("xfm", [D, BUF])
    inp("maskl", [NSSM, 128, 23])
    inp("wi", [NSSM, D, 2176])
    inp("cwr", [NSSM, 128, 64])
    inp("cbr", [NSSM, 128, 16])
    inp("wo", [NSSM, D_INNER, D])
    inp("lng", [NSSM, 128, 8])
    inp("lnb", [NSSM, 128, 8])
    inp("dec2", [64, 512])
    inp("dechalf", [64, 512])
    inp("interdec", [64, 512])
    inp("m64", [64, 64])
    inp("maskS", [64, 64])
    inp("bdcol", [128])
    out_xn0 = nc.dram_tensor("xn0b", [D, TPC], BF16, kind="ExternalOutput").ap()
    dbg_xb = nc.dram_tensor("dbg_xb", [128, 4096], BF16, kind="ExternalOutput").ap()
    dbg_ut = nc.dram_tensor("dbg_ut", [128, 8192], BF16, kind="ExternalOutput").ap()
    dbg_yf = nc.dram_tensor("dbg_yf", [128, 8192], BF16, kind="ExternalOutput").ap()
    dbg_zs = nc.dram_tensor("dbg_zs", [128, 4096], F32, kind="ExternalOutput").ap()
    dbg_sr = nc.dram_tensor("dbg_sr", [4, 512], F32, kind="ExternalOutput").ap()
    out_l5b = nc.dram_tensor("xl5b", [D, TPC], BF16, kind="ExternalOutput").ap()
    out_l5f = nc.dram_tensor("xl5f", [D, TPC], F32, kind="ExternalOutput").ap()

    es = contextlib.ExitStack()
    with es:
        def sb(name, shape, dt):
            return es.enter_context(nc.sbuf_tensor(name, shape, dt))

        X = sb("X", [128, 8 * BUF], F32)
        Xb = sb("Xb", [128, 8 * 512], BF16)
        WIS = [sb(f"WIS{i}", [128, 8 * 128], BF16) for i in range(2)]
        WOS = [sb(f"WOS{i}", [128, 16 * 128], BF16) for i in range(2)]
        XS = [sb(f"XS{i}", [128, 515], BF16) for i in range(2)]
        UF = [sb(f"UF{i}", [128, 512], BF16) for i in range(2)]
        ACC = sb("ACC", [128, 512], F32)
        UT = sb("UT", [128, 4 * 2048], BF16)
        Bf = sb("Bf", [64, 512], BF16)
        Cf = sb("Cf", [64, 512], BF16)
        BS = sb("BS", [64, 512], BF16)
        CS = sb("CS", [64, 512], BF16)
        CI = sb("CI", [64, 512], BF16)
        BCT = sb("BCT", [128, 4 * 64], BF16)
        BD = sb("BD", [128, 4 * 64], BF16)
        SF = [sb(f"SF{i}", [128, 128], BF16) for i in range(2)]
        H = [sb(f"H{i}", [64, 2048], BF16) for i in range(2)]
        YF = sb("YF", [128, 16 * 512], BF16)
        ZS = sb("ZS", [128, 8 * 512], F32)
        ZQ = sb("ZQ", [128, 512], F32)
        ZQ2 = sb("ZQ2", [128, 512], F32)
        SR0 = sb("SR0", [1, 512], F32)
        SR1 = sb("SR1", [1, 512], F32)
        SR2 = sb("SR2", [1, 512], F32)
        SR3 = sb("SR3", [1, 512], F32)
        CW = sb("CW", [128, 64], F32)
        CB = sb("CB", [128, 16], F32)
        LG = sb("LG", [128, 8], F32)
        LB = sb("LB", [128, 8], F32)
        MSK = sb("MSK", [128, 23], F32)
        ONES = sb("ONES", [128, 1], F32)
        ONESB = sb("ONESB", [128, 1], BF16)
        ONESR = sb("ONESR", [1, 128], F32)
        DEC2 = sb("DEC2", [64, 512], F32)
        DECH = sb("DECH", [64, 512], F32)
        INTD = sb("INTD", [64, 512], F32)
        M64 = sb("M64", [64, 64], F32)
        MASKS = sb("MASKS", [64, 64], F32)
        BDC = sb("BDC", [128, 1], F32)
        XT = sb("XT", [128, 16 * 3], BF16)

        PM = es.enter_context(nc.psum_tensor("PM", [128, 2048], F32))
        PA = es.enter_context(nc.psum_tensor("PA", [128, 2048], F32))

        s = Sched(nc)

        for dc in range(8):
            s.dma("sync", X[:, dc * BUF:(dc + 1) * BUF],
                  ap["xfm"][dc * 128:(dc + 1) * 128, :], writes=[("X", dc, sg) for sg in range(6)],
                  stream="init")
        for nm, buf in [("dec2", DEC2), ("dechalf", DECH), ("interdec", INTD),
                        ("m64", M64), ("maskS", MASKS)]:
            s.dma("sync", buf[:, :], ap[nm][:, :], writes=[nm], stream="init")
        s.dma("sync", BDC[:, :], ap["bdcol"].rearrange("(p o) -> p o", o=1), writes=["bdc"],
              stream="init")
        s.op("vector", lambda E: E.memset(ONES[:, :], 1.0), writes=["ones"])
        s.op("vector", lambda E: E.memset(ONESR[:, :], 1.0), writes=["onesr"])
        s.op("vector", lambda E: E.memset(XT[:, :], 0.0), writes=[("XT", dd) for dd in range(16)])

        def fm_ln(c0, n, tag, affine, zs_keys=None):
            src = (lambda dc: ZS[:, dc * 512:dc * 512 + n]) if zs_keys else \
                  (lambda dc: X[:, dc * BUF + c0: dc * BUF + c0 + n])
            xkeys = (lambda dc: zs_keys(dc)) if zs_keys else \
                    (lambda dc: [("X", dc, sg) for sg in range(c0 // 512, (c0 + n + 511) // 512)])
            for dc in range(8):
                s.op("tensor", lambda E, dc=dc: E.matmul(
                    PA[0:1, 1024:1024 + n], ONES[:, :], src(dc),
                    start=(dc == 0), stop=(dc == 7)),
                    reads=xkeys(dc) + ["ones"], writes=pk("PA", 1024, 1024 + n))
                s.op("scalar", lambda E, dc=dc: E.activation(ZQ[:, 0:n], src(dc), AF.Square),
                     reads=xkeys(dc), writes=["zq"])
                s.op("tensor", lambda E, dc=dc: E.matmul(
                    PA[0:1, 1536:1536 + n], ONES[:, :], ZQ[:, 0:n],
                    start=(dc == 0), stop=(dc == 7)),
                    reads=["zq", "ones"], writes=pk("PA", 1536, 1536 + n))
            s.op("vector", lambda E: E.tensor_copy(SR0[0:1, 0:n], PA[0:1, 1024:1024 + n]),
                 reads=pk("PA", 1024, 1024 + n), writes=["sr0"])
            s.op("vector", lambda E: E.tensor_copy(SR1[0:1, 0:n], PA[0:1, 1536:1536 + n]),
                 reads=pk("PA", 1536, 1536 + n), writes=["sr1"])
            s.op("vector", lambda E: E.tensor_scalar_mul(SR3[0:1, 0:n], SR0[0:1, 0:n], 1.0 / D),
                 reads=["sr0"], writes=["sr3"])
            s.op("vector", lambda E: E.tensor_scalar_mul(SR1[0:1, 0:n], SR1[0:1, 0:n], 1.0 / D),
                 reads=["sr1"], writes=["sr1"])
            s.op("vector", lambda E: E.tensor_mul(SR2[0:1, 0:n], SR3[0:1, 0:n], SR3[0:1, 0:n]),
                 reads=["sr3"], writes=["sr2"])
            s.op("vector", lambda E: E.tensor_sub(SR1[0:1, 0:n], SR1[0:1, 0:n], SR2[0:1, 0:n]),
                 reads=["sr1", "sr2"], writes=["sr1"])
            s.op("vector", lambda E: E.tensor_scalar_add(SR1[0:1, 0:n], SR1[0:1, 0:n], LN_EPS),
                 reads=["sr1"], writes=["sr1"])
            s.op("scalar", lambda E: E.activation(SR2[0:1, 0:n], SR1[0:1, 0:n], AF.Ln),
                 reads=["sr1"], writes=["sr2"])
            s.op("scalar", lambda E: E.activation(SR2[0:1, 0:n], SR2[0:1, 0:n], AF.Exp,
                                                  scale=-0.5),
                 reads=["sr2"], writes=["sr2"])
            s.op("vector", lambda E: E.tensor_mul(SR3[0:1, 0:n], SR3[0:1, 0:n], SR2[0:1, 0:n]),
                 reads=["sr2", "sr3"], writes=["sr3"])
            s.op("tensor", lambda E: E.matmul(PA[:, 0:n], ONESR[:, :], SR2[0:1, 0:n],
                                              start=True, stop=True),
                 reads=["sr2", "onesr"], writes=pk("PA", 0, n))
            s.op("tensor", lambda E: E.matmul(PA[:, 512:512 + n], ONESR[:, :], SR3[0:1, 0:n],
                                              start=True, stop=True),
                 reads=["sr3", "onesr"], writes=pk("PA", 512, 512 + n))
            outkeys = lambda dc: [("X", dc, sg) for sg in range(c0 // 512, (c0 + n + 511) // 512)]
            for dc in range(8):
                s.op("vector", lambda E, dc=dc: E.tensor_mul(ZQ2[:, 0:n], src(dc), PA[:, 0:n]),
                     reads=xkeys(dc) + pk("PA", 0, n), writes=["zq2"])
                if affine:
                    s.op("vector", lambda E, dc=dc: E.tensor_sub(ZQ2[:, 0:n], ZQ2[:, 0:n],
                                                                 PA[:, 512:512 + n]),
                         reads=["zq2"] + pk("PA", 512, 512 + n), writes=["zq2"])
                    s.op("vector", lambda E, dc=dc: E.tensor_scalar(
                        X[:, dc * BUF + c0: dc * BUF + c0 + n], ZQ2[:, 0:n],
                        LG[:, dc:dc + 1], LB[:, dc:dc + 1], ALU.mult, ALU.add),
                        reads=["zq2", "lg", "lb"], writes=outkeys(dc))
                else:
                    s.op("vector", lambda E, dc=dc: E.tensor_sub(
                        X[:, dc * BUF + c0: dc * BUF + c0 + n], ZQ2[:, 0:n],
                        PA[:, 512:512 + n]),
                        reads=["zq2"] + pk("PA", 512, 512 + n), writes=outkeys(dc))

        for gi in range(6):
            c0 = gi * 512
            n = min(512, BUF - c0)
            fm_ln(c0, n, ("xn0", gi), affine=False)
        for dc in range(8):
            s.dma("gpsimd", out_xn0[dc * 128:(dc + 1) * 128, :],
                  X[:, dc * BUF + REAL0: dc * BUF + REAL1],
                  reads=[("X", dc, sg) for sg in range(1, 6)], stream="out")

        def emit_group(ell, off, gc0, ncg):
            tg0 = off + gc0 * 128
            tl = ncg * 128
            sgs = list(range(tg0 // 512, (tg0 + tl + 511) // 512))
            gid = (ell, gc0)
            if "proj" not in PHASES:
                return
            _p_load = "noload" not in PHASES
            _p_mm = "nomm" not in PHASES
            _p_xb = "noxb" not in PHASES
            _p_bc = "nobc" not in PHASES

            for dc in range(8):
                s.op("vector", lambda E, dc=dc: E.tensor_copy(
                    Xb[:, dc * 512: dc * 512 + tl], X[:, dc * BUF + tg0: dc * BUF + tg0 + tl]),
                    reads=[("X", dc, sg) for sg in sgs], writes=[("Xb", dc)])

            if ell == 0 and gc0 == 0 and "dbg" in PHASES:
                s.dma("sync", dbg_xb[:, :], Xb[:, 0:4096],
                      reads=[("Xb", dc) for dc in range(8)], stream="out")
            for do in range(17):
                wb = WIS[do % 2]
                s.dma("gpsimd", wb[:, :].rearrange("p (c o) -> p c o", c=8),
                      ap["wi"][ell][:, do * 128:(do + 1) * 128]
                      .rearrange("(c p) o -> p c o", p=128),
                      writes=[("WIS", do % 2)], stream=f"wi{do % 2}")
                pcol = (do % 2) * 512
                for di in range(8):
                    s.op("tensor", lambda E, do=do, di=di, pcol=pcol, wb=wb: E.matmul(
                        PM[:, pcol:pcol + tl], wb[:, di * 128:(di + 1) * 128],
                        Xb[:, di * 512: di * 512 + tl],
                        start=(di == 0), stop=(di == 7)),
                        reads=[("WIS", do % 2), ("Xb", di)], writes=pk("PM", pcol, pcol + tl))
                if do < 16 and "conv" not in PHASES:
                    continue
                if do < 16:
                    xb = XS[do % 2]
                    s.op("vector", lambda E, do=do, xb=xb: E.tensor_copy(
                        xb[:, 0:3], XT[:, do * 3: do * 3 + 3]),
                        reads=[("XT", do)], writes=[("XS", do % 2)])
                    s.op("scalar", lambda E, pcol=pcol, xb=xb: E.copy(
                        xb[:, 3:3 + tl], PM[:, pcol:pcol + tl]),
                        reads=pk("PM", pcol, pcol + tl), writes=[("XS", do % 2)])
                    s.op("vector", lambda E, do=do, xb=xb: E.tensor_copy(
                        XT[:, do * 3: do * 3 + 3], xb[:, tl: tl + 3]),
                        reads=[("XS", do % 2)], writes=[("XT", do)])
                    s.op("vector", lambda E, do=do, xb=xb: E.tensor_scalar_mul(
                        ACC[:, 0:tl], xb[:, 0:tl], CW[:, do * 4: do * 4 + 1]),
                        reads=[("XS", do % 2), "cw"], writes=["acc"])
                    for k2 in (1, 2, 3):
                        s.op("vector", lambda E, do=do, k2=k2, xb=xb: E.scalar_tensor_tensor(
                            ACC[:, 0:tl], xb[:, k2:k2 + tl], CW[:, do * 4 + k2: do * 4 + k2 + 1],
                            ACC[:, 0:tl], ALU.mult, ALU.add),
                            reads=[("XS", do % 2), "cw", "acc"], writes=["acc"])
                    uf = UF[do % 2]
                    s.op("scalar", lambda E, do=do, uf=uf: E.activation(
                        uf[:, 0:tl], ACC[:, 0:tl], AF.Silu, bias=CB[:, do:do + 1]),
                        reads=["acc", "cb"], writes=[("UF", do % 2)])
                    for c in range(ncg):
                        q = "sync"
                        s.dmat(q, UT[:, (c * 16 + do) * 128:(c * 16 + do) * 128 + 128],
                               uf[:, c * 128:(c + 1) * 128],
                               reads=[("UF", do % 2)], writes=[("UT", c, do)],
                               stream=f"ut{c}", note=f"ut {ell} {gc0} {c} {do} tl{tl}")
                else:
                    s.op("vector", lambda E, pcol=pcol: E.tensor_copy(
                        Bf[:, 0:tl], PM[0:64, pcol:pcol + tl]),
                        reads=pk("PM", pcol, pcol + tl), writes=["bfr"])
                    s.op("vector", lambda E, pcol=pcol: E.tensor_copy(
                        Cf[:, 0:tl], PM[64:128, pcol:pcol + tl]),
                        reads=pk("PM", pcol, pcol + tl), writes=["cfr"])

            if "scan" not in PHASES:
                return
            _s_bct = "nobct" not in PHASES
            _s_sc = "nosc" not in PHASES
            _s_y = "noy" not in PHASES
            _s_dh = "nodh" not in PHASES
            _s_he = "nohe" not in PHASES
            _s_ye = "noye" not in PHASES
            s.op("vector", lambda E: E.tensor_mul(BS[:, 0:tl], Bf[:, 0:tl], DEC2[:, 0:tl]),
                 reads=["bfr", "dec2"], writes=["bs"])
            s.op("vector", lambda E: E.tensor_mul(CS[:, 0:tl], Cf[:, 0:tl], DECH[:, 0:tl]),
                 reads=["cfr", "dechalf"], writes=["cs"])
            s.op("vector", lambda E: E.tensor_mul(CI[:, 0:tl], Cf[:, 0:tl], INTD[:, 0:tl]),
                 reads=["cfr", "interdec"], writes=["ci"])
            for c in range(ncg if _s_bct else 0):
                s.dmat("sync", BCT[:, c * 64:(c + 1) * 64], Bf[:, c * 128:(c + 1) * 128],
                       reads=["bfr"], writes=[("bct", c)], stream=f"bct{c}", note=f"bct {c}")
                s.op("vector", lambda E, c=c: E.tensor_scalar_mul(
                    BD[:, c * 64:(c + 1) * 64], BCT[:, c * 64: c * 64 + 64], BDC[:, 0:1]),
                    reads=[("bct", c), "bdc"], writes=[("bd", c)])

            for c in range(ncg):
                ci = gc0 + c
                sf = SF[c % 2]
                if off + ci * 128 < REAL0:
                    s.op("vector", lambda E, c=c, ci=ci: E.tensor_scalar_mul(
                        UT[:, c * 2048:(c + 1) * 2048], UT[:, c * 2048:(c + 1) * 2048],
                        MSK[:, ci:ci + 1]),
                        reads=[("UT", c, dd) for dd in range(16)] + ["msk"],
                        writes=[("UT", c, dd) for dd in range(16)])
                subs = [(0, 0, 0), (1, 1, 64), (0, 1, 128)] if _s_sc else []
                for (jb, tb, pc) in subs:
                    s.op("tensor", lambda E, c=c, jb=jb, tb=tb, pc=pc: E.matmul(
                        PA[0:64, pc:pc + 64], BS[:, c * 128 + jb * 64: c * 128 + jb * 64 + 64],
                        CS[:, c * 128 + tb * 64: c * 128 + tb * 64 + 64],
                        start=True, stop=True),
                        reads=["bs", "cs"], writes=pk("PA", pc, pc + 64))
                if _s_sc:
                    s.op("vector", lambda E, sf=sf: E.tensor_mul(sf[0:64, 0:64], PA[0:64, 0:64], MASKS[:, :]),
                         reads=pk("PA", 0, 64) + ["maskS"], writes=[("sf", c % 2)])
                    s.op("vector", lambda E, sf=sf: E.tensor_mul(sf[64:128, 64:128], PA[0:64, 64:128], MASKS[:, :]),
                         reads=pk("PA", 64, 128) + ["maskS"], writes=[("sf", c % 2)])
                    s.op("vector", lambda E, sf=sf: E.tensor_mul(sf[0:64, 64:128], PA[0:64, 128:192], M64[:, :]),
                         reads=pk("PA", 128, 192) + ["m64"], writes=[("sf", c % 2)])
                    s.op("vector", lambda E, sf=sf: E.memset(sf[64:128, 0:64], 0.0),
                         writes=[("sf", c % 2)])
                hb = H[(ci + 1) % 2]
                for dcc in range(16 if _s_y else 0):
                    s.op("tensor", lambda E, c=c, dcc=dcc, sf=sf, ci=ci: E.matmul(
                        PM[:, dcc * 128:(dcc + 1) * 128],
                        UT[:, (c * 16 + dcc) * 128:(c * 16 + dcc) * 128 + 128],
                        sf[:, :], start=True, stop=(ci == 0)),
                        reads=[("UT", c, dcc), ("sf", c % 2)],
                        writes=pk("PM", dcc * 128, (dcc + 1) * 128))
                    if ci > 0:
                        hs = hb[:, dcc * 128:(dcc + 1) * 128]
                        s.op("tensor", lambda E, c=c, dcc=dcc, hs=hs: E.matmul(
                            PM[:, dcc * 128:(dcc + 1) * 128], hs,
                            CI[:, c * 128:(c + 1) * 128], start=False, stop=True),
                            reads=[("H", (ci + 1) % 2), "ci"],
                            writes=pk("PM", dcc * 128, (dcc + 1) * 128))
                for q in range(4 if (_s_dh and _s_bct) else 0):
                    pcq = q * 512
                    s.op("tensor", lambda E, c=c, q=q, pcq=pcq: E.matmul(
                        PA[0:64, pcq:pcq + 512], BD[:, c * 64:(c + 1) * 64],
                        UT[:, c * 2048 + q * 512: c * 2048 + (q + 1) * 512],
                        start=True, stop=True),
                        reads=[("bd", c)] + [("UT", c, dd) for dd in range(q * 4, q * 4 + 4)],
                        writes=pk("PA", pcq, pcq + 512))
                for dcc in range(16 if (_s_ye and _s_y) else 0):
                    eng = "vector" if (dcc < 8 or "allv" in PHASES) else "scalar"
                    def yev(E, c=c, dcc=dcc, eng=eng):
                        dst = YF[:, dcc * 512 + c * 128: dcc * 512 + (c + 1) * 128]
                        srcp = PM[:, dcc * 128:(dcc + 1) * 128]
                        return E.tensor_copy(dst, srcp) if eng == "vector" else E.copy(dst, srcp)
                    s.op(eng, yev, reads=pk("PM", dcc * 128, (dcc + 1) * 128),
                         writes=[("YF", dcc, c)])
                hn = H[ci % 2]
                if not (_s_he and _s_dh and _s_bct):
                    continue
                s.op("vector", lambda E, hn=hn: E.tensor_copy(hn[:, 0:1024], PA[0:64, 0:1024]),
                     reads=pk("PA", 0, 1024), writes=[("H", ci % 2)])
                s.op("scalar", lambda E, hn=hn: E.copy(hn[:, 1024:2048], PA[0:64, 1024:2048]),
                     reads=pk("PA", 1024, 2048), writes=[("H", ci % 2)])

            if ell == 0 and gc0 == 0 and "dbg" in PHASES:
                s.dma("sync", dbg_ut[:, :], UT[:, 0:8192],
                      reads=[("UT", cc, dd) for cc in range(4) for dd in range(16)], stream="out")
                s.dma("sync", dbg_yf[:, :], YF[:, 0:8192],
                      reads=[("YF", dd, cc) for dd in range(16) for cc in range(4)], stream="out")
            if "outproj" not in PHASES:
                return
            for dm in range(8):
                wb = WOS[dm % 2]
                s.dma("gpsimd", wb[:, :].rearrange("p (c o) -> p c o", c=16),
                      ap["wo"][ell][:, dm * 128:(dm + 1) * 128]
                      .rearrange("(c p) o -> p c o", p=128),
                      writes=[("WOS", dm % 2)], stream=f"wo{dm % 2}")
                pcol = 1024 + (dm % 2) * 512
                for di in range(16):
                    s.op("tensor", lambda E, dm=dm, di=di, pcol=pcol, wb=wb: E.matmul(
                        PM[:, pcol:pcol + tl], wb[:, di * 128:(di + 1) * 128],
                        YF[:, di * 512: di * 512 + tl],
                        start=(di == 0), stop=(di == 15)),
                        reads=[("WOS", dm % 2)] + [("YF", di, cc) for cc in range(ncg)],
                        writes=pk("PM", pcol, pcol + tl))
                s.op("vector", lambda E, dm=dm, pcol=pcol: E.tensor_add(
                    ZS[:, dm * 512: dm * 512 + tl], PM[:, pcol:pcol + tl],
                    X[:, dm * BUF + tg0: dm * BUF + tg0 + tl]),
                    reads=pk("PM", pcol, pcol + tl) + [("X", dm, sg) for sg in sgs],
                    writes=[("ZS", dm)])
            if ell == 0 and gc0 == 0 and "dbg" in PHASES:
                s.dma("sync", dbg_zs[:, :], ZS[:, 0:4096],
                      reads=[("ZS", dm) for dm in range(8)], stream="out")
            if "ln" in PHASES:
                fm_ln(tg0, tl, gid, affine=True,
                      zs_keys=lambda dc: [("ZS", dc)])
                if ell == 0 and gc0 == 0 and "dbg" in PHASES:
                    s.dma("sync", dbg_sr[0:1, :], SR0[0:1, :], reads=["sr0"], stream="out")
                    s.dma("sync", dbg_sr[1:2, :], SR1[0:1, :], reads=["sr1"], stream="out")
                    s.dma("sync", dbg_sr[2:3, :], SR2[0:1, :], reads=["sr2"], stream="out")
                    s.dma("sync", dbg_sr[3:4, :], SR3[0:1, :], reads=["sr3"], stream="out")

        for ell in range(NSSM):
            off = OFFS[ell]
            s.dma("sync", CW[:, :], ap["cwr"][ell], writes=["cw"], stream="cw")
            s.dma("sync", CB[:, :], ap["cbr"][ell], writes=["cb"], stream="cb")
            s.dma("sync", LG[:, :], ap["lng"][ell], writes=["lg"], stream="lg")
            s.dma("sync", LB[:, :], ap["lnb"][ell], writes=["lb"], stream="lb")
            nch = NCHUNK[ell]
            s.dma("sync", MSK[:, 0:nch], ap["maskl"][ell][:, 0:nch], writes=["msk"], stream="msk")
            s.op("vector", lambda E: E.memset(XT[:, :], 0.0),
                 writes=[("XT", dd) for dd in range(16)])

            for (gc0, ncg) in ssm_groups(ell):
                emit_group(ell, off, gc0, ncg)


        for dc in range(8):
            s.dma("gpsimd", out_l5b[dc * 128:(dc + 1) * 128, :],
                  X[:, dc * BUF + REAL0: dc * BUF + REAL1],
                  reads=[("X", dc, sg) for sg in range(1, 6)], stream="out")
            s.dma("sync", out_l5f[dc * 128:(dc + 1) * 128, :],
                  X[:, dc * BUF + REAL0: dc * BUF + REAL1],
                  reads=[("X", dc, sg) for sg in range(1, 6)], stream="out")

        s.emit(es.enter_context(nc.Block()))
    return nc


def _consts_A():
    j = np.arange(512)
    dec2 = np.tile(((-2.0) ** (j % 64))[None, :].astype(np.float64), (64, 1)).astype(np.float32)
    dechalf = np.tile(((-0.5) ** (j % 64))[None, :], (64, 1)).astype(np.float32)
    interdec = np.tile(((-0.5) ** ((j % 128) + 1))[None, :], (64, 1)).astype(np.float32)
    m64 = np.full((64, 64), 2.0 ** -64, np.float32)
    maskS = np.triu(np.ones((64, 64), np.float32))
    p = np.arange(128)
    bdcol = ((-0.5) ** (127 - (p % 128))).astype(np.float32)
    return dict(dec2=dec2, dechalf=dechalf, interdec=interdec, m64=m64,
                maskS=maskS, bdcol=bdcol)


def prep_launch_A(inputs):
    x = np.asarray(inputs["L0_attn_x"], np.float32)
    wi = np.asarray(inputs["L0_attn_in_proj_w"], np.float32)
    cw = np.asarray(inputs["L0_attn_conv_w"], np.float32)
    cb = np.asarray(inputs["L0_attn_conv_b"], np.float32)
    wo = np.asarray(inputs["L0_attn_out_proj_w"], np.float32)
    g = np.asarray(inputs["L0_attn_ssm_ln_g"], np.float32)
    bb = np.asarray(inputs["L0_attn_ssm_ln_b"], np.float32)
    consts = _consts_A()
    nw = cw.shape[0]
    cwr = cw.reshape(nw, 16, 128, 4).transpose(0, 2, 1, 3).reshape(nw, 128, 64)
    cbr = cb.reshape(nw, 16, 128).transpose(0, 2, 1)
    lng = g.reshape(nw, 8, 128).transpose(0, 2, 1)
    lnb = bb.reshape(nw, 8, 128).transpose(0, 2, 1)
    maps = []
    for c in range(NCORES):
        b, r0 = c // 4, (c % 4) * TPC
        lo, hi = r0 - EXT, r0 - EXT + BUF
        xs = np.zeros((BUF, D), np.float32)
        msk = np.zeros((BUF,), np.float32)
        a, bnd = max(0, lo), min(T, hi)
        xs[a - lo:bnd - lo] = x[b, a:bnd]
        msk[a - lo:bnd - lo] = 1.0
        maskl = np.zeros((NSSM, 128, 23), np.float32)
        for ell in range(NSSM):
            for ci in range(NCHUNK[ell]):
                r = OFFS[ell] + ci * 128
                maskl[ell, :, ci] = msk[r:r + 128]
        m = dict(consts)
        m.update(xfm=np.ascontiguousarray(xs.T), maskl=maskl, wi=wi, cwr=cwr, cbr=cbr,
                 wo=wo, lng=lng, lnb=lnb)
        maps.append({k: _c(v) for k, v in m.items()})
    return maps


def _c(v):
    return np.ascontiguousarray(v, dtype=np.float32)


def run_launch_A(inputs):
    nc = build_launch_A()
    maps = prep_launch_A(inputs)
    res = run_bass_kernel_spmd(nc, maps, core_ids=list(range(NCORES)))
    return res.results


NQT = T // 512
NKC = T // 128


def build_launch_B():
    nc = bass.Bass("TRN2", target_bir_lowering=False, debug=False)
    ap = {}
    ap["xl5fm"] = nc.dram_tensor("xl5fm", [D, T], BF16, kind="ExternalInput").ap()
    ap["xn0fm"] = nc.dram_tensor("xn0fm", [D, T], BF16, kind="ExternalInput").ap()
    for nm in ("wq_p", "wk_p", "wv_p", "fq_p", "fk_p", "fv_p"):
        ap[nm] = nc.dram_tensor(nm, [2, D, 128], F32, kind="ExternalInput").ap()
    ap["cmask"] = nc.dram_tensor("cmask", [4, 128, 512], F32, kind="ExternalInput").ap()
    out_a = nc.dram_tensor("attn_o", [4, 64, T], BF16, kind="ExternalOutput").ap()
    out_f = nc.dram_tensor("ffn_o", [4, 64, T], BF16, kind="ExternalOutput").ap()

    es = contextlib.ExitStack()
    with es:
        def sb(name, shape, dt):
            return es.enter_context(nc.sbuf_tensor(name, shape, dt))

        XB = sb("XB", [128, 8 * 2048], BF16)
        QAB = sb("QAB", [128, T], BF16)
        KAB = sb("KAB", [128, T], BF16)
        VT = sb("VT", [128, NKC * 128], BF16)
        VF = sb("VF", [128, 2048], BF16)
        V1 = [sb(f"V1_{i}", [128, NKC * 65], BF16) for i in range(2)]
        WP = sb("WP", [128, 8 * 128], BF16)
        SQ = sb("SQ", [128, T], BF16)
        PT = sb("PT", [128, 2048], BF16)
        SC = sb("SC", [128, 2048], BF16)
        OF = sb("OF", [64, T], BF16)
        OS = sb("OS", [64, 512], BF16)
        ZT = sb("ZT", [128, 8], F32)
        ZC = [sb(f"ZC{i}", [128, NKC], F32) for i in range(2)]
        BIAS2 = [sb(f"BIAS2_{i}", [128, NKC], F32) for i in range(2)]
        NM = [sb(f"NM{i}", [128, 1], F32) for i in range(2)]
        MQ = sb("MQ", [1, 8], F32)
        RZ = sb("RZ", [1, 512], F32)
        CM = sb("CM", [128, 4 * 512], F32)
        ONESC = sb("ONESC", [128, 1], BF16)
        ONESR = sb("ONESR", [1, 128], F32)

        PM = es.enter_context(nc.psum_tensor("PM", [128, 2048], F32))
        PA = es.enter_context(nc.psum_tensor("PA", [128, 2048], F32))

        s = Sched(nc)
        s.dma("sync", CM[:, :].rearrange("p (m q) -> p m q", m=4),
              ap["cmask"].rearrange("m p q -> p m q"),
              writes=["cm"], stream="init")
        s.op("vector", lambda E: E.memset(ONESC[:, :], 1.0), writes=["onesc"])
        s.op("vector", lambda E: E.memset(ONESR[:, :], 1.0), writes=["onesr"])

        def qkv_phase(xap, wq, wk, wv, pair, qscale):
            for qb in range(4):
                for dc in range(8):
                    s.dma("gpsimd", XB[:, dc * 2048:(dc + 1) * 2048],
                          xap[dc * 128:(dc + 1) * 128, qb * 2048:(qb + 1) * 2048],
                          writes=[("XB", dc)], stream="xb")
                for wi_, wap, tag in ((0, wq, "q"), (1, wk, "k"), (2, wv, "v")):
                    s.dma("gpsimd", WP[:, :].rearrange("p (c o) -> p c o", c=8),
                          wap[pair].rearrange("(c p) o -> p c o", p=128),
                          writes=["wp"], stream="wp")
                    for qt4 in range(4):
                        for dc in range(8):
                            s.op("tensor", lambda E, qt4=qt4, dc=dc: E.matmul(
                                PM[:, qt4 * 512:(qt4 + 1) * 512],
                                WP[:, dc * 128:(dc + 1) * 128],
                                XB[:, dc * 2048 + qt4 * 512: dc * 2048 + (qt4 + 1) * 512],
                                start=(dc == 0), stop=(dc == 7)),
                                reads=["wp", ("XB", dc)],
                                writes=pk("PM", qt4 * 512, (qt4 + 1) * 512))
                    for qt4 in range(4):
                        col = qb * 2048 + qt4 * 512
                        if tag == "q":
                            s.op("scalar", lambda E, qt4=qt4, col=col: E.mul(
                                QAB[:, col:col + 512], PM[:, qt4 * 512:(qt4 + 1) * 512], qscale),
                                reads=pk("PM", qt4 * 512, (qt4 + 1) * 512), writes=[("QAB", col // 2048)])
                        elif tag == "k":
                            s.op("scalar", lambda E, qt4=qt4, col=col: E.copy(
                                KAB[:, col:col + 512], PM[:, qt4 * 512:(qt4 + 1) * 512]),
                                reads=pk("PM", qt4 * 512, (qt4 + 1) * 512), writes=[("KAB", col // 2048)])
                        else:
                            s.op("vector", lambda E, qt4=qt4: E.tensor_copy(
                                VF[:, qt4 * 512:(qt4 + 1) * 512], PM[:, qt4 * 512:(qt4 + 1) * 512]),
                                reads=pk("PM", qt4 * 512, (qt4 + 1) * 512), writes=["vf"])
                for c in range(16):
                    s.dmat("sync", VT[:, (qb * 16 + c) * 128:(qb * 16 + c) * 128 + 128],
                           VF[:, c * 128:(c + 1) * 128],
                           reads=["vf"], writes=[("VT", qb * 16 + c)], stream="vt")

        def stage_v1():
            for h in range(2):
                s.op("vector", lambda E, h=h: E.memset(V1[h][:, :], 1.0),
                     writes=[("V1", h)])
                for kc in range(NKC):
                    s.op("vector", lambda E, h=h, kc=kc: E.tensor_copy(
                        V1[h][:, kc * 65: kc * 65 + 64], VT[:, kc * 128 + h * 64: kc * 128 + h * 64 + 64]),
                        reads=[("VT", kc)], writes=[("V1", h)])

        def head_norm_max(buf, half, dsti):
            s.op("scalar", lambda E: E.activation(SQ[:, :], buf[:, :], AF.Square),
                 reads=[(buf.name, qb) for qb in range(4)], writes=["sq"])
            s.op("vector", lambda E: E.memset(MQ[0:1, dsti:dsti + 1], 0.0), writes=[("mq", dsti)])
            for t in range(NQT):
                s.op("tensor", lambda E, t=t: E.matmul(
                    PA[0:1, 1024:1536], ONESC[half * 64:half * 64 + 64, :],
                    SQ[half * 64:half * 64 + 64, t * 512:(t + 1) * 512],
                    start=True, stop=True),
                    reads=["sq", "onesc"], writes=pk("PA", 1024, 1536))
                s.op("vector", lambda E, t=t: E.reduce_max(
                    MQ[0:1, 6:7], PA[0:1, 1024:1536], axis=mybir.AxisListType.X),
                    reads=pk("PA", 1024, 1536), writes=[("mq", 6)])
                s.op("vector", lambda E: E.tensor_max(
                    MQ[0:1, dsti:dsti + 1], MQ[0:1, dsti:dsti + 1], MQ[0:1, 6:7]),
                    reads=[("mq", 6), ("mq", dsti)], writes=[("mq", dsti)])

        def make_nm(h, qi, ki, extra_scale=1.0):
            s.op("vector", lambda E: E.tensor_mul(MQ[0:1, 7:8], MQ[0:1, qi:qi + 1], MQ[0:1, ki:ki + 1]),
                 reads=[("mq", qi), ("mq", ki)], writes=[("mq", 7)])
            s.op("scalar", lambda E: E.activation(MQ[0:1, 7:8], MQ[0:1, 7:8], AF.Ln),
                 reads=[("mq", 7)], writes=[("mq", 7)])
            s.op("scalar", lambda E: E.activation(MQ[0:1, 7:8], MQ[0:1, 7:8], AF.Exp, scale=0.5),
                 reads=[("mq", 7)], writes=[("mq", 7)])
            s.op("vector", lambda E: E.tensor_scalar_mul(MQ[0:1, 7:8], MQ[0:1, 7:8], -float(extra_scale)),
                 reads=[("mq", 7)], writes=[("mq", 7)])
            s.op("tensor", lambda E: E.matmul(PA[:, 1536:1537], ONESR[:, :], MQ[0:1, 7:8],
                                              start=True, stop=True),
                 reads=[("mq", 7), "onesr"], writes=pk("PA", 1536, 1537))
            s.op("vector", lambda E, h=h: E.tensor_copy(NM[h][:, :], PA[:, 1536:1537]),
                 reads=pk("PA", 1536, 1537), writes=[("nm", h)])

        def causal_head(h, pair):
            hb = h * 64
            for qt in range(NQT):
                nkc = 4 * qt + 4
                for kc in range(nkc):
                    s.op("tensor", lambda E, kc=kc, qt=qt: E.matmul(
                        PM[:, 0:512], KAB[hb:hb + 64, kc * 128:(kc + 1) * 128],
                        QAB[hb:hb + 64, qt * 512:(qt + 1) * 512],
                        start=True, stop=True),
                        reads=[("KAB", kc // 16), ("QAB", qt // 4)], writes=pk("PM", 0, 512))
                    s.op("scalar", lambda E, kc=kc: E.activation(
                        PT[:, 0:512], PM[:, 0:512], AF.Exp, bias=NM[h][:, 0:1]),
                        reads=pk("PM", 0, 512) + [("nm", h)], writes=["pt"])
                    if kc >= 4 * qt:
                        j = kc - 4 * qt
                        s.op("vector", lambda E, j=j: E.tensor_mul(
                            PT[:, 0:512], PT[:, 0:512], CM[:, j * 512:(j + 1) * 512]),
                            reads=["pt", "cm"], writes=["pt"])
                    s.op("tensor", lambda E, kc=kc, nkc=nkc: E.matmul(
                        PA[0:65, 0:512], V1[h][:, kc * 65:(kc + 1) * 65], PT[:, 0:512],
                        start=(kc == 0), stop=(kc == nkc - 1)),
                        reads=["pt", ("V1", h)], writes=pk("PA", 0, 512))
                s.op("scalar", lambda E: E.copy(OS[:, :], PA[0:64, 0:512]),
                     reads=pk("PA", 0, 512), writes=["os"])
                s.op("vector", lambda E: E.reciprocal(RZ[0:1, :], PA[64:65, 0:512]),
                     reads=pk("PA", 0, 512) + ["os"], writes=["rz"])
                s.op("tensor", lambda E: E.matmul(PA[:, 512:1024], ONESR[:, :], RZ[0:1, :],
                                                  start=True, stop=True),
                     reads=["rz", "onesr"], writes=pk("PA", 512, 1024))
                s.op("vector", lambda E, qt=qt: E.tensor_mul(
                    OF[:, qt * 512:(qt + 1) * 512], OS[:, :], PA[0:64, 512:1024]),
                    reads=["os"] + pk("PA", 512, 1024), writes=[("of", qt)])
            s.dma("gpsimd", out_a[pair * 2 + h], OF[:, :],
                  reads=[("of", qt) for qt in range(NQT)], stream="out")

        def ffn_head_pass1(h):
            hb = h * 64
            for kc in range(NKC):
                for g in range(4):
                    for q4 in range(4):
                        qt = g * 4 + q4
                        s.op("tensor", lambda E, kc=kc, qt=qt, q4=q4: E.matmul(
                            PM[:, q4 * 512:(q4 + 1) * 512],
                            KAB[hb:hb + 64, kc * 128:(kc + 1) * 128],
                            QAB[hb:hb + 64, qt * 512:(qt + 1) * 512],
                            start=True, stop=True),
                            reads=[("KAB", kc // 16), ("QAB", qt // 4)],
                            writes=pk("PM", q4 * 512, (q4 + 1) * 512))
                    s.op("scalar", lambda E, g=g: E.activation(
                        SC[:, :], PM[:, 0:2048], AF.Exp, bias=NM[h][:, 0:1],
                        accum_out=ZT[:, g:g + 1]),
                        reads=pk("PM", 0, 2048) + [("nm", h)], writes=["scx", ("zt", g)])
                s.op("vector", lambda E, kc=kc, h=h: E.reduce_sum(
                    ZC[h][:, kc:kc + 1], ZT[:, 0:4], axis=mybir.AxisListType.X),
                    reads=[("zt", g) for g in range(4)], writes=[("zc", h)])
            s.op("scalar", lambda E, h=h: E.activation(BIAS2[h][:, :], ZC[h][:, :], AF.Ln),
                 reads=[("zc", h)], writes=[("b2", h)])
            s.op("vector", lambda E, h=h: E.tensor_scalar(
                BIAS2[h][:, :], BIAS2[h][:, :], -1.0, NM[h][:, 0:1], ALU.mult, ALU.add),
                reads=[("b2", h), ("nm", h)], writes=[("b2", h)])

        def ffn_head_pass2(h, pair):
            hb = h * 64
            for qtg in range(NQT // 2):
                for kc in range(NKC):
                    for q2 in range(2):
                        qt = qtg * 2 + q2
                        s.op("tensor", lambda E, kc=kc, qt=qt, q2=q2: E.matmul(
                            PM[:, q2 * 512:(q2 + 1) * 512],
                            KAB[hb:hb + 64, kc * 128:(kc + 1) * 128],
                            QAB[hb:hb + 64, qt * 512:(qt + 1) * 512],
                            start=True, stop=True),
                            reads=[("KAB", kc // 16), ("QAB", qt // 4)],
                            writes=pk("PM", q2 * 512, (q2 + 1) * 512))
                    s.op("scalar", lambda E, kc=kc, h=h: E.activation(
                        PT[:, 0:1024], PM[:, 0:1024], AF.Exp, bias=BIAS2[h][:, kc:kc + 1]),
                        reads=pk("PM", 0, 1024) + [("b2", h)], writes=["pt"])
                    for q2 in range(2):
                        s.op("tensor", lambda E, kc=kc, q2=q2, h=h: E.matmul(
                            PA[0:64, q2 * 512:(q2 + 1) * 512],
                            VT[:, kc * 128 + h * 64: kc * 128 + h * 64 + 64],
                            PT[:, q2 * 512:(q2 + 1) * 512],
                            start=(kc == 0), stop=(kc == NKC - 1)),
                            reads=["pt", ("VT", kc)],
                            writes=pk("PA", q2 * 512, (q2 + 1) * 512))
                for q2 in range(2):
                    qt = qtg * 2 + q2
                    s.op("scalar", lambda E, qt=qt, q2=q2: E.copy(
                        OF[:, qt * 512:(qt + 1) * 512], PA[0:64, q2 * 512:(q2 + 1) * 512]),
                        reads=pk("PA", q2 * 512, (q2 + 1) * 512), writes=[("of", qt)])
            s.dma("gpsimd", out_f[pair * 2 + h], OF[:, :],
                  reads=[("of", qt) for qt in range(NQT)], stream="out")

        for pair in range(2):
            qkv_phase(ap["xl5fm"], ap["wq_p"], ap["wk_p"], ap["wv_p"], pair, 0.125)
            stage_v1()
            head_norm_max(QAB, 0, 0)
            head_norm_max(QAB, 1, 1)
            head_norm_max(KAB, 0, 2)
            head_norm_max(KAB, 1, 3)
            make_nm(0, 0, 2)
            make_nm(1, 1, 3)
            for h in range(2):
                causal_head(h, pair)
            qkv_phase(ap["xn0fm"], ap["fq_p"], ap["fk_p"], ap["fv_p"], pair, 0.03125)
            head_norm_max(QAB, 0, 0)
            head_norm_max(QAB, 1, 1)
            head_norm_max(KAB, 0, 2)
            head_norm_max(KAB, 1, 3)
            make_nm(0, 0, 2)
            make_nm(1, 1, 3)
            for h in range(2):
                ffn_head_pass1(h)
            for h in range(2):
                ffn_head_pass2(h, pair)

        s.emit(es.enter_context(nc.Block()))
    return nc


def prep_launch_B(inputs, xl5fm_full, xn0fm_full):
    j = np.arange(512)[None, :]
    k = np.arange(128)[:, None]
    cmask = np.stack([(j >= 128 * m + k).astype(np.float32) for m in range(4)])
    def pack(w):
        w = np.asarray(w, np.float32)
        return w
    wq, wk, wv = (pack(inputs[f"L0_attn_w{n}"]) for n in "qkv")
    fq, fk, fv = (pack(inputs[f"L0_ffn_W{n}"]) for n in ("q", "k", "v"))
    maps = []
    for c in range(NCORES):
        b, hq = c // 4, 4 * (c % 4)
        def pairs(w):
            return np.stack([
                np.concatenate([w[:, (hq + 2 * p) * 64:(hq + 2 * p) * 64 + 64],
                                w[:, (hq + 2 * p + 1) * 64:(hq + 2 * p + 1) * 64 + 64]], axis=1)
                for p in range(2)])
        m = dict(xl5fm=xl5fm_full[b], xn0fm=xn0fm_full[b], cmask=cmask,
                 wq_p=pairs(wq), wk_p=pairs(wk), wv_p=pairs(wv),
                 fq_p=pairs(fq), fk_p=pairs(fk), fv_p=pairs(fv))
        maps.append(m)
    return maps


def build_launch_C():
    nc = bass.Bass("TRN2", target_bir_lowering=False, debug=False)
    ap = {}

    def inp(name, shape, dt=F32):
        ap[name] = nc.dram_tensor(name, shape, dt, kind="ExternalInput").ap()

    inp("oafm", [D, TPC], BF16)
    inp("offm", [D, TPC], BF16)
    inp("xl5f", [D, TPC])
    inp("xfm0", [D, TPC])
    inp("wo", [D, D])
    inp("fwo", [D, D])
    inp("fwff", [D, D])
    inp("alng", [128, 8])
    inp("alnb", [128, 8])
    inp("wkT", [D, D])
    inp("wvT", [D, D])
    inp("bkr", [128, 8])
    inp("bvr", [128, 8])
    inp("projT", [64, 128])
    out_x1 = nc.dram_tensor("x1f", [D, TPC], F32, kind="ExternalOutput").ap()
    out_xn1 = nc.dram_tensor("xn1b", [D, TPC], BF16, kind="ExternalOutput").ap()
    out_kvt = nc.dram_tensor("kvt", [16, 128, 65], F32, kind="ExternalOutput").ap()

    es = contextlib.ExitStack()
    with es:
        def sb(name, shape, dt):
            return es.enter_context(nc.sbuf_tensor(name, shape, dt))

        X1 = sb("X1", [128, 8 * TPC], F32)
        OA = sb("OA", [128, 8 * TPC], BF16)
        ZB = sb("ZB", [128, 8 * TPC], BF16)
        WS = [sb(f"WS{i}", [128, 8 * 128], BF16) for i in range(2)]
        RS = sb("RS", [128, 512], F32)
        ZQ = sb("ZQ", [128, 512], F32)
        ZQ2 = sb("ZQ2", [128, 512], F32)
        SR0 = sb("SR0", [1, 512], F32)
        SR1 = sb("SR1", [1, 512], F32)
        SR2 = sb("SR2", [1, 512], F32)
        SR3 = sb("SR3", [1, 512], F32)
        LG = sb("LG", [128, 8], F32)
        LB = sb("LB", [128, 8], F32)
        BK = sb("BK", [128, 8], F32)
        BV = sb("BV", [128, 8], F32)
        ONES = sb("ONES", [128, 1], F32)
        ONESB = sb("ONESB", [128, 1], BF16)
        ONESR = sb("ONESR", [1, 128], F32)
        PHR = sb("PHR", [128, 512], F32)
        PRJ = sb("PRJ", [128, 128], F32)
        KF = sb("KF", [128, TPC], BF16)
        KT = sb("KT", [128, 16 * 128], BF16)
        V1 = sb("V1", [128, 16 * 80], BF16)
        KVF = sb("KVF", [128, 65], F32)
        BKS = sb("BKS", [128, 1], F32)

        PM = es.enter_context(nc.psum_tensor("PM", [128, 2048], F32))
        PA = es.enter_context(nc.psum_tensor("PA", [128, 2048], F32))

        s = Sched(nc)
        s.op("vector", lambda E: E.memset(ONES[:, :], 1.0), writes=["ones"])
        s.op("vector", lambda E: E.memset(ONESB[:, :], 1.0), writes=["ones"])
        s.op("vector", lambda E: E.memset(ONESR[:, :], 1.0), writes=["onesr"])
        for nm, buf in (("alng", LG), ("alnb", LB), ("bkr", BK), ("bvr", BV)):
            s.dma("sync", buf[:, :], ap[nm][:, :], writes=[nm[:2]], stream="par")

        def colnorm(tag, n, src, skeys, dst, dkeys, mode, affine=None, eps=LN_EPS,
                    src_bf=False):
            onesv = ONESB if src_bf else ONES
            for dc in range(8):
                s.op("tensor", lambda E, dc=dc: E.matmul(
                    PA[0:1, 1024:1024 + n], onesv[:, :], src(dc),
                    start=(dc == 0), stop=(dc == 7)),
                    reads=skeys(dc) + ["ones"], writes=pk("PA", 1024, 1024 + n))
                s.op("scalar", lambda E, dc=dc: E.activation(ZQ[:, 0:n], src(dc), AF.Square),
                     reads=skeys(dc), writes=["zq"])
                s.op("tensor", lambda E, dc=dc: E.matmul(
                    PA[0:1, 1536:1536 + n], ONES[:, :], ZQ[:, 0:n],
                    start=(dc == 0), stop=(dc == 7)),
                    reads=["zq", "ones"], writes=pk("PA", 1536, 1536 + n))
            if mode == "ln":
                s.op("vector", lambda E: E.tensor_copy(SR0[0:1, 0:n], PA[0:1, 1024:1024 + n]),
                     reads=pk("PA", 1024, 1024 + n), writes=["sr0"])
            s.op("vector", lambda E: E.tensor_copy(SR1[0:1, 0:n], PA[0:1, 1536:1536 + n]),
                 reads=pk("PA", 1536, 1536 + n), writes=["sr1"])
            if mode == "ln":
                s.op("vector", lambda E: E.tensor_scalar_mul(SR3[0:1, 0:n], SR0[0:1, 0:n], 1.0 / D),
                     reads=["sr0"], writes=["sr3"])
                s.op("vector", lambda E: E.tensor_scalar_mul(SR1[0:1, 0:n], SR1[0:1, 0:n], 1.0 / D),
                     reads=["sr1"], writes=["sr1"])
                s.op("vector", lambda E: E.tensor_mul(SR2[0:1, 0:n], SR3[0:1, 0:n], SR3[0:1, 0:n]),
                     reads=["sr3"], writes=["sr2"])
                s.op("vector", lambda E: E.tensor_sub(SR1[0:1, 0:n], SR1[0:1, 0:n], SR2[0:1, 0:n]),
                     reads=["sr1", "sr2"], writes=["sr1"])
                s.op("vector", lambda E: E.tensor_scalar_add(SR1[0:1, 0:n], SR1[0:1, 0:n], eps),
                     reads=["sr1"], writes=["sr1"])
                s.op("scalar", lambda E: E.activation(SR2[0:1, 0:n], SR1[0:1, 0:n], AF.Ln),
                     reads=["sr1"], writes=["sr2"])
                s.op("scalar", lambda E: E.activation(SR2[0:1, 0:n], SR2[0:1, 0:n], AF.Exp, scale=-0.5),
                     reads=["sr2"], writes=["sr2"])
                s.op("vector", lambda E: E.tensor_mul(SR3[0:1, 0:n], SR3[0:1, 0:n], SR2[0:1, 0:n]),
                     reads=["sr2", "sr3"], writes=["sr3"])
            else:
                s.op("scalar", lambda E: E.activation(SR2[0:1, 0:n], SR1[0:1, 0:n], AF.Ln),
                     reads=["sr1"], writes=["sr2"])
                s.op("scalar", lambda E: E.activation(SR2[0:1, 0:n], SR2[0:1, 0:n], AF.Exp, scale=0.5),
                     reads=["sr2"], writes=["sr2"])
                s.op("vector", lambda E: E.tensor_scalar_max(SR2[0:1, 0:n], SR2[0:1, 0:n], 1e-12),
                     reads=["sr2"], writes=["sr2"])
                s.op("vector", lambda E: E.reciprocal(SR2[0:1, 0:n], SR2[0:1, 0:n]),
                     reads=["sr2"], writes=["sr2"])
            s.op("tensor", lambda E: E.matmul(PA[:, 0:n], ONESR[:, :], SR2[0:1, 0:n],
                                              start=True, stop=True),
                 reads=["sr2", "onesr"], writes=pk("PA", 0, n))
            if mode == "ln":
                s.op("tensor", lambda E: E.matmul(PA[:, 512:512 + n], ONESR[:, :], SR3[0:1, 0:n],
                                                  start=True, stop=True),
                     reads=["sr3", "onesr"], writes=pk("PA", 512, 512 + n))
            for dc in range(8):
                s.op("vector", lambda E, dc=dc: E.tensor_mul(ZQ2[:, 0:n], src(dc), PA[:, 0:n]),
                     reads=skeys(dc) + pk("PA", 0, n), writes=["zq2"])
                if mode == "ln":
                    s.op("vector", lambda E, dc=dc: E.tensor_sub(ZQ2[:, 0:n], ZQ2[:, 0:n],
                                                                 PA[:, 512:512 + n]),
                         reads=["zq2"] + pk("PA", 512, 512 + n), writes=["zq2"])
                if affine is not None:
                    g, bcol = affine
                    s.op("vector", lambda E, dc=dc: E.tensor_scalar(
                        dst(dc), ZQ2[:, 0:n], g[:, dc:dc + 1], bcol[:, dc:dc + 1],
                        ALU.mult, ALU.add),
                        reads=["zq2", "al"], writes=dkeys(dc))
                else:
                    s.op("vector", lambda E, dc=dc: E.tensor_copy(dst(dc), ZQ2[:, 0:n]),
                         reads=["zq2"], writes=dkeys(dc))

        def proj_block(wap, rhs, rhs_keys, evac, ttl=512):
            for ic in range(8):
                wb = WS[ic % 2]
                s.dma("gpsimd", wb[:, :].rearrange("p (c o) -> p c o", c=8),
                      wap[:, ic * 128:(ic + 1) * 128].rearrange("(c p) o -> p c o", p=128),
                      writes=[("WS", ic % 2)], stream=f"ws{ic % 2}")
                for tt in range(TPC // ttl):
                    pcol = (tt % 2) * 512
                    for dc in range(8):
                        s.op("tensor", lambda E, ic=ic, tt=tt, dc=dc, pcol=pcol, wb=wb: E.matmul(
                            PM[:, pcol:pcol + ttl], wb[:, dc * 128:(dc + 1) * 128],
                            rhs(dc, tt), start=(dc == 0), stop=(dc == 7)),
                            reads=[("WS", ic % 2)] + rhs_keys(dc, tt),
                            writes=pk("PM", pcol, pcol + ttl))
                    evac(ic, tt, pcol)

        oa_keys = lambda dc, tt: [("OA", dc)]
        for dc in range(8):
            s.dma("sync", OA[:, dc * TPC:(dc + 1) * TPC],
                  ap["oafm"][dc * 128:(dc + 1) * 128, :], writes=[("OA", dc)], stream="oa")

        def evac_z1(ic, tt, pcol):
            s.dma("sync", RS[:, 0:512], ap["xl5f"][ic * 128:(ic + 1) * 128, tt * 512:(tt + 1) * 512],
                  writes=["rs"], stream="rs")
            s.op("vector", lambda E, ic=ic, tt=tt, pcol=pcol: E.tensor_add(
                X1[:, ic * TPC + tt * 512: ic * TPC + tt * 512 + 512],
                PM[:, pcol:pcol + 512], RS[:, 0:512]),
                reads=pk("PM", pcol, pcol + 512) + ["rs"], writes=[("X1", ic, tt)])
        proj_block(ap["wo"], lambda dc, tt: OA[:, dc * TPC + tt * 512: dc * TPC + tt * 512 + 512],
                   oa_keys, evac_z1)
        for tt in range(4):
            colnorm(("z1", tt), 512,
                    lambda dc, tt=tt: X1[:, dc * TPC + tt * 512: dc * TPC + tt * 512 + 512],
                    lambda dc, tt=tt: [("X1", dc, tt)],
                    lambda dc, tt=tt: X1[:, dc * TPC + tt * 512: dc * TPC + tt * 512 + 512],
                    lambda dc, tt=tt: [("X1", dc, tt)],
                    "ln", affine=(LG, LB))

        for dc in range(8):
            s.dma("sync", OA[:, dc * TPC:(dc + 1) * TPC],
                  ap["offm"][dc * 128:(dc + 1) * 128, :],
                  writes=[("OA", dc)], stream="oa")

        ZB2 = sb("ZB2", [128, 8 * TPC], BF16)

        def evac_zb(buf, bkey):
            def f(ic, tt, pcol):
                s.op("scalar", lambda E, ic=ic, tt=tt, pcol=pcol: E.copy(
                    buf[:, ic * TPC + tt * 512: ic * TPC + tt * 512 + 512], PM[:, pcol:pcol + 512]),
                    reads=pk("PM", pcol, pcol + 512), writes=[(bkey, ic)])
            return f
        proj_block(ap["fwo"], lambda dc, tt: OA[:, dc * TPC + tt * 512: dc * TPC + tt * 512 + 512],
                   oa_keys, evac_zb(ZB, "ZB"))
        for tt in range(4):
            colnorm(("l2a", tt), 512,
                    lambda dc, tt=tt: ZB[:, dc * TPC + tt * 512: dc * TPC + tt * 512 + 512],
                    lambda dc, tt=tt: [("ZB", dc)],
                    lambda dc, tt=tt: ZB[:, dc * TPC + tt * 512: dc * TPC + tt * 512 + 512],
                    lambda dc, tt=tt: [("ZB", dc)],
                    "l2", src_bf=True)
        proj_block(ap["fwff"], lambda dc, tt: ZB[:, dc * TPC + tt * 512: dc * TPC + tt * 512 + 512],
                   lambda dc, tt: [("ZB", dc)], evac_zb(ZB2, "ZB2"))
        for tt in range(4):
            colnorm(("l2b", tt), 512,
                    lambda dc, tt=tt: ZB2[:, dc * TPC + tt * 512: dc * TPC + tt * 512 + 512],
                    lambda dc, tt=tt: [("ZB2", dc)],
                    lambda dc, tt=tt: ZB2[:, dc * TPC + tt * 512: dc * TPC + tt * 512 + 512],
                    lambda dc, tt=tt: [("ZB2", dc)],
                    "l2", src_bf=True)
        for dc in range(8):
            for tt in range(4):
                cs = slice(dc * TPC + tt * 512, dc * TPC + tt * 512 + 512)
                s.op("scalar", lambda E, cs=cs: E.activation(ZB2[:, cs], ZB2[:, cs], AF.Gelu),
                     reads=[("ZB2", dc)], writes=[("ZB2", dc)])
                s.dma("sync", RS[:, 0:512],
                      ap["xfm0"][dc * 128:(dc + 1) * 128, tt * 512:(tt + 1) * 512],
                      writes=["rs"], stream="rs")
                s.op("vector", lambda E, cs=cs: E.tensor_add(X1[:, cs], X1[:, cs], RS[:, 0:512]),
                     reads=[("X1", dc, tt), "rs"], writes=[("X1", dc, tt)])
                s.op("vector", lambda E, cs=cs: E.tensor_add(X1[:, cs], X1[:, cs], ZB2[:, cs]),
                     reads=[("X1", dc, tt), ("ZB2", dc)], writes=[("X1", dc, tt)])
                s.dma("gpsimd", out_x1[dc * 128:(dc + 1) * 128, tt * 512:(tt + 1) * 512],
                      X1[:, cs], reads=[("X1", dc, tt)], stream="out")
        for tt in range(4):
            colnorm(("xn1", tt), 512,
                    lambda dc, tt=tt: X1[:, dc * TPC + tt * 512: dc * TPC + tt * 512 + 512],
                    lambda dc, tt=tt: [("X1", dc, tt)],
                    lambda dc, tt=tt: ZB[:, dc * TPC + tt * 512: dc * TPC + tt * 512 + 512],
                    lambda dc, tt=tt: [("ZB", dc)],
                    "ln")
        for dc in range(8):
            s.dma("gpsimd", out_xn1[dc * 128:(dc + 1) * 128, :], ZB[:, dc * TPC:(dc + 1) * TPC],
                  reads=[("ZB", dc)], stream="out")

        s.dma("sync", PRJ[0:64, 0:128], ap["projT"][:, :], writes=["prj"], stream="par")
        s.op("vector", lambda E: E.memset(PRJ[64:128, 0:128], -0.5), writes=["prj"])
        KJ = sb("KJ", [128, TPC], F32)
        VJ = sb("VJ", [128, TPC], BF16)
        for hp in range(8):
            for which, wap, bap, dstb, dkey in (("k", ap["wkT"], BK, KJ, "kj"),
                                                ("v", ap["wvT"], BV, VJ, "vj")):
                wb = WS[0] if which == "k" else WS[1]
                s.dma("gpsimd", wb[:, :].rearrange("p (c o) -> p c o", c=8),
                      wap[:, hp * 128:(hp + 1) * 128].rearrange("(c p) o -> p c o", p=128),
                      writes=[("WS", 0 if which == "k" else 1)], stream=f"ws{0 if which == 'k' else 1}")
                for tt in range(4):
                    pcol = (tt % 2) * 512
                    for dc in range(8):
                        s.op("tensor", lambda E, tt=tt, dc=dc, pcol=pcol, wb=wb: E.matmul(
                            PM[:, pcol:pcol + 512], wb[:, dc * 128:(dc + 1) * 128],
                            ZB[:, dc * TPC + tt * 512: dc * TPC + tt * 512 + 512],
                            start=(dc == 0), stop=(dc == 7)),
                            reads=[("WS", 0 if which == "k" else 1), ("ZB", dc)],
                            writes=pk("PM", pcol, pcol + 512))
                    if which == "k":
                        s.op("vector", lambda E, hp=hp: E.tensor_scalar_mul(
                            BKS[:, 0:1], BK[:, hp:hp + 1], 0.3535533905932738),
                            reads=["bk"], writes=["zqb"])
                        s.op("vector", lambda E, tt=tt, pcol=pcol: E.tensor_scalar(
                            KJ[:, tt * 512:(tt + 1) * 512], PM[:, pcol:pcol + 512],
                            0.3535533905932738, BKS[:, 0:1], ALU.mult, ALU.add),
                            reads=pk("PM", pcol, pcol + 512) + ["zqb"], writes=["kj"])
                    else:
                        s.op("vector", lambda E, tt=tt, pcol=pcol, hp=hp: E.tensor_scalar_add(
                            VJ[:, tt * 512:(tt + 1) * 512], PM[:, pcol:pcol + 512],
                            BV[:, hp:hp + 1]),
                            reads=pk("PM", pcol, pcol + 512) + ["bv"], writes=["vj"])
            for h in range(2):
                hb = h * 64
                for tt in range(4):
                    ts_ = slice(tt * 512, (tt + 1) * 512)
                    s.op("vector", lambda E, hb=hb, ts_=ts_: E.tensor_copy(
                        PHR[0:64, :], KJ[hb:hb + 64, ts_]),
                        reads=["kj"], writes=["phr"])
                    s.op("scalar", lambda E, hb=hb, ts_=ts_: E.activation(
                        PHR[64:128, :], KJ[hb:hb + 64, ts_], AF.Square),
                        reads=["kj"], writes=["phr"])
                    s.op("tensor", lambda E, tt=tt: E.matmul(
                        PA[:, 0:512], PRJ[:, 0:128], PHR[:, 0:512],
                        start=True, stop=True),
                        reads=["phr", "prj"], writes=pk("PA", 0, 512))
                    s.op("scalar", lambda E, tt=tt: E.activation(
                        KF[:, tt * 512:(tt + 1) * 512], PA[:, 0:512], AF.Exp),
                        reads=pk("PA", 0, 512), writes=["kff"])
                s.op("vector", lambda E: E.tensor_scalar(
                    KF[:, :], KF[:, :], 0.08838834764831845, 1e-6, ALU.mult, ALU.add),
                    reads=["kff"], writes=["kff"])
                s.op("vector", lambda E: E.memset(V1[:, :], 1.0), writes=["v1"])
                for c in range(16):
                    s.dmat("sync", KT[:, c * 128:(c + 1) * 128], KF[:, c * 128:(c + 1) * 128],
                           reads=["kff"], writes=[("kt", c)], stream="kt")
                for c in range(16):
                    s.dmat("sync", V1[:, c * 80: c * 80 + 64],
                           VJ[hb:hb + 64, c * 128:(c + 1) * 128],
                           reads=["vj"], writes=["v1"], stream="v1")
                for c in range(16):
                    s.op("tensor", lambda E, c=c: E.matmul(
                        PA[:, 512:512 + 65], KT[:, c * 128:(c + 1) * 128], V1[:, c * 80: c * 80 + 65],
                        start=(c == 0), stop=(c == 15)),
                        reads=[("kt", c), "v1"], writes=pk("PA", 512, 1024))
                s.op("vector", lambda E: E.tensor_copy(KVF[:, :], PA[:, 512:512 + 65]),
                     reads=pk("PA", 512, 1024), writes=["kvf"])
                s.dma("gpsimd", out_kvt[hp * 2 + h], KVF[:, :], reads=["kvf"], stream="out")

        s.emit(es.enter_context(nc.Block()))
    return nc


def build_launch_D():
    nc = bass.Bass("TRN2", target_bir_lowering=False, debug=False)
    ap = {}

    def inp(name, shape, dt=F32):
        ap[name] = nc.dram_tensor(name, shape, dt, kind="ExternalInput").ap()

    inp("xn1fm", [D, TPC], BF16)
    inp("x1tm", [TPC, D])
    inp("kvp", [3, 16, 128, 65])
    inp("wqT", [D, D])
    inp("wkT", [D, D])
    inp("wvT", [D, D])
    inp("woT", [D, D])
    inp("bqr", [128, 8])
    inp("bkr", [128, 8])
    inp("bvr", [128, 8])
    inp("borow", [1, D])
    inp("projT", [64, 128])
    inp("rw", [D, 64])
    inp("rbrow", [1, 64])
    inp("w1", [64, D, 256])
    inp("b1r", [128, 2, 64])
    inp("w2r", [64 * 256, D])
    inp("b2", [64, D])
    inp("sw1", [2, D, 2048])
    inp("sb1r", [128, 16, 2])
    inp("sw2r", [2 * 2048, D])
    inp("sb2", [2, D])
    inp("sels", [64, 64 * 128], BF16)
    inp("ident", [128, 128], BF16)
    inp("triu", [128, 128], BF16)
    out_o = nc.dram_tensor("outtm", [TPC, D], F32, kind="ExternalOutput").ap()

    es = contextlib.ExitStack()
    with es:
        def sb(name, shape, dt):
            return es.enter_context(nc.sbuf_tensor(name, shape, dt))

        XN = sb("XN", [128, 8 * TPC], BF16)
        OATT = sb("OATT", [128, 8 * TPC], BF16)
        SLAB = sb("SLAB", [128, 128 * 256], BF16)
        SLAB2 = sb("SLAB2", [128, 16 * 256], BF16)
        SELB = [sb(f"SELB{i}", [64, 128], BF16) for i in range(2)]
        QF = [sb(f"QF{i}", [128, TPC], BF16) for i in range(2)]
        KF = [sb(f"KF{i}", [128, TPC], BF16) for i in range(2)]
        VJ = sb("VJ", [128, TPC], BF16)
        KT = sb("KT", [128, 16 * 128], BF16)
        V1 = sb("V1", [128, 16 * 80], BF16)
        SST = sb("SST", [128, 16 * 65], F32)
        SBF = sb("SBF", [128, 65], BF16)
        CW = sb("CW", [128, 64], BF16)
        CWF = sb("CWF", [64, 16 * 128], BF16)
        PHR = sb("PHR", [128, 256], F32)
        PRJ = sb("PRJ", [128, 128], F32)
        TRIU = sb("TRIU", [128, 128], BF16)
        IDENT = sb("IDENT", [128, 128], BF16)
        W1S = [sb(f"W1S{i}", [128, 8 * 256], BF16) for i in range(2)]
        WS = [W1S[0], W1S[1]]
        W2S = [sb(f"W2S{i}", [128, 1024], BF16) for i in range(2)]
        WOTS = [W2S[0], W2S[1]]
        RW = sb("RW", [128, 8 * 64], BF16)
        RBB = sb("RBB", [128, 64], F32)
        BOS = sb("BOS", [1, 1024], BF16)
        ONESRB = sb("ONESRB", [1, 128], BF16)
        SB2A = sb("SB2A", [1, 1024], BF16)
        SB2B = sb("SB2B", [1, 1024], BF16)
        B2S = sb("B2S", [64, 1024], BF16)
        LGT = sb("LGT", [128, 64], F32)
        PRB = sb("PRB", [128, 64], F32)
        PMK = sb("PMK", [128, 64], F32)
        SELM = sb("SELM", [128, 64], F32)
        COLS = sb("COLS", [128, 8], F32)
        BQ = sb("BQ", [128, 8], F32)
        BK = sb("BK", [128, 8], F32)
        BV = sb("BV", [128, 8], F32)
        B1C = sb("B1C", [128, 2 * 64], F32)
        SB1C = sb("SB1C", [128, 16 * 2], F32)
        BIASC = sb("BIASC", [128, 1], F32)
        ATS = sb("ATS", [128, 128], BF16)
        OYS = sb("OYS", [64, 128], BF16)
        DENS = sb("DENS", [1, 128], F32)
        OUT = sb("OUT", [128, 512], F32)
        X1S = sb("X1S", [128, 512], F32)
        ONES = sb("ONES", [128, 1], F32)
        ONESR = sb("ONESR", [1, 128], F32)

        PM = es.enter_context(nc.psum_tensor("PM", [128, 2048], F32))
        PA = es.enter_context(nc.psum_tensor("PA", [128, 1536], F32))
        PB16 = es.enter_context(nc.psum_tensor("PB16", [128, 1024], BF16))

        s = Sched(nc)
        s.op("vector", lambda E: E.memset(ONES[:, :], 1.0), writes=["ones"])
        s.op("vector", lambda E: E.memset(ONESR[:, :], 1.0), writes=["onesr"])
        s.op("vector", lambda E: E.memset(ONESRB[:, :], 1.0), writes=["onesr"])
        for dc in range(8):
            s.dma("sync", XN[:, dc * TPC:(dc + 1) * TPC],
                  ap["xn1fm"][dc * 128:(dc + 1) * 128, :], writes=[("XN", dc)], stream="init")
        s.dma("sync", PRJ[0:64, :], ap["projT"][:, :], writes=["prj"], stream="init")
        s.op("vector", lambda E: E.memset(PRJ[64:128, :], -0.5), writes=["prj"])
        s.dma("sync", TRIU[:, :], ap["triu"][:, :], writes=["triu"], stream="init")
        s.dma("sync", IDENT[:, :], ap["ident"][:, :], writes=["ident"], stream="init")
        for nm, buf in (("bqr", BQ), ("bkr", BK), ("bvr", BV)):
            s.dma("sync", buf[:, :], ap[nm][:, :], writes=[nm], stream="init")
        s.dma("gpsimd", BOS[:, :], ap["borow"][:, :], writes=["bos"], stream="init2")
        s.dma("gpsimd", SB2A[:, :], ap["sb2"][0:1, :], writes=["sb2a"], stream="init2")
        s.dma("gpsimd", SB2B[:, :], ap["sb2"][1:2, :], writes=["sb2b"], stream="init2")
        s.dma("gpsimd", B2S[:, :], ap["b2"][:, :], writes=["b2s"], stream="init2")
        s.dma("gpsimd", RW[:, :].rearrange("p (c o) -> p c o", c=8),
              ap["rw"].rearrange("(c p) o -> p c o", p=128), writes=["rw"], stream="init2")
        s.dma("gpsimd", B1C[:, :].rearrange("p (a b) -> p a b", a=2),
              ap["b1r"][:, :, :], writes=["b1c"], stream="init2")
        s.dma("gpsimd", SB1C[:, :].rearrange("p (a b) -> p a b", a=16),
              ap["sb1r"][:, :, :], writes=["sb1c"], stream="init2")
        for h in range(16):
            s.dma("sync", SST[:, h * 65:(h + 1) * 65], ap["kvp"][0, h],
                  reads=["kvchain"], writes=[("sst", h)], stream="kvp")
            for p in (1, 2):
                s.dma("sync", PHR[:, 0:65], ap["kvp"][p, h],
                      reads=["kvchain"], writes=["phr"], stream="kvp")
                s.op("vector", lambda E, h=h: E.tensor_add(
                    SST[:, h * 65:(h + 1) * 65], SST[:, h * 65:(h + 1) * 65], PHR[:, 0:65]),
                    reads=[("sst", h), "phr"], writes=[("sst", h), "kvchain"])

        for hp in range(8):
            for wi_, (wnm, bbuf, bnm) in enumerate((("wqT", BQ, "bqr"), ("wkT", BK, "bkr"),
                                                    ("wvT", BV, "bvr"))):
                wb = WS[wi_ % 2]
                s.dma("gpsimd", wb[:, 0:1024].rearrange("p (c o) -> p c o", c=8),
                      ap[wnm][:, hp * 128:(hp + 1) * 128].rearrange("(c p) o -> p c o", p=128),
                      writes=[("W1S", wi_ % 2)], stream=f"w1s{wi_ % 2}")
                for tt in range(4):
                    pcol = (wi_ % 2) * 512
                    for dc in range(8):
                        s.op("tensor", lambda E, tt=tt, dc=dc, pcol=pcol, wb=wb: E.matmul(
                            PM[:, pcol:pcol + 512], wb[:, dc * 128:(dc + 1) * 128],
                            XN[:, dc * TPC + tt * 512: dc * TPC + tt * 512 + 512],
                            start=(dc == 0), stop=(dc == 7)),
                            reads=[("W1S", wi_ % 2), ("XN", dc)],
                            writes=pk("PM", pcol, pcol + 512))
                    if wi_ == 2:
                        s.op("vector", lambda E, tt=tt, pcol=pcol, hp=hp: E.tensor_scalar_add(
                            VJ[:, tt * 512:(tt + 1) * 512], PM[:, pcol:pcol + 512],
                            BV[:, hp:hp + 1]),
                            reads=pk("PM", pcol, pcol + 512) + ["bvr"], writes=["vjj"])
                    else:
                        for h in range(2):
                            hb = h * 64
                            dstf = (QF if wi_ == 0 else KF)[h]
                            dkey = ("qf" if wi_ == 0 else "kf", h)
                            if h == 0:
                                s.op("vector", lambda E, hp=hp, bbuf=bbuf: E.tensor_scalar(
                                    BIASC[:, 0:1], bbuf[:, hp:hp + 1], 0.3535533905932738, None,
                                    ALU.mult),
                                    reads=[bnm], writes=["biasc"])
                            for sh in range(2):
                                s.op("vector", lambda E, pcol=pcol, hb=hb, sh=sh: E.tensor_scalar(
                                    PHR[0:64, :], PM[hb:hb + 64, pcol + sh * 256: pcol + sh * 256 + 256],
                                    0.3535533905932738,
                                    BIASC[hb:hb + 64, 0:1] if hb else BIASC[0:64, 0:1],
                                    ALU.mult, ALU.add),
                                    reads=pk("PM", pcol, pcol + 512) + ["biasc"], writes=["phr"])
                                s.op("scalar", lambda E: E.activation(
                                    PHR[64:128, :], PHR[0:64, :], AF.Square),
                                    reads=["phr"], writes=["phr"])
                                s.op("tensor", lambda E, sh=sh: E.matmul(
                                    PA[:, 0:256], PRJ[:, :], PHR[:, :], start=True, stop=True),
                                    reads=["phr", "prj"], writes=pk("PA", 0, 256))
                                s.op("scalar", lambda E, tt=tt, dstf=dstf, sh=sh: E.activation(
                                    dstf[:, tt * 512 + sh * 256: tt * 512 + sh * 256 + 256],
                                    PA[:, 0:256], AF.Exp),
                                    reads=pk("PA", 0, 256), writes=[dkey])
            for h in range(2):
                for f, key in ((QF[h], ("qf", h)), (KF[h], ("kf", h))):
                    s.op("vector", lambda E, f=f: E.tensor_scalar(
                        f[:, :], f[:, :], 0.08838834764831845, 1e-6, ALU.mult, ALU.add),
                        reads=[key], writes=[key])
            for h in range(2):
                hd = hp * 2 + h
                hb = h * 64
                s.op("vector", lambda E: E.memset(V1[:, :], 1.0), writes=["v1"])
                for c in range(16):
                    s.dmat("sync", KT[:, c * 128:(c + 1) * 128],
                           KF[h][:, c * 128:(c + 1) * 128],
                           reads=[("kf", h)], writes=[("kt", c)], stream="kt")
                    s.dmat("sync", V1[:, c * 80: c * 80 + 64],
                           VJ[hb:hb + 64, c * 128:(c + 1) * 128],
                           reads=["vjj"], writes=["v1"], stream="v1")
                for c in range(16):
                    s.op("vector", lambda E, hd=hd: E.tensor_copy(SBF[:, :], SST[:, hd * 65:(hd + 1) * 65]),
                         reads=[("sst", hd)], writes=["sbf"])
                    s.op("tensor", lambda E, c=c, h=h: E.matmul(
                        PA[:, 512:640], KF[h][:, c * 128:(c + 1) * 128],
                        QF[h][:, c * 128:(c + 1) * 128], start=True, stop=True),
                        reads=[("kf", h), ("qf", h)], writes=pk("PA", 512, 640))
                    s.op("vector", lambda E: E.tensor_mul(ATS[:, :], PA[:, 512:640], TRIU[:, :]),
                         reads=pk("PA", 512, 640) + ["triu"], writes=["ats"])
                    s.op("tensor", lambda E, c=c, h=h: E.matmul(
                        PA[0:65, 640:768], SBF[:, :], QF[h][:, c * 128:(c + 1) * 128],
                        start=True, stop=False),
                        reads=["sbf", ("qf", h)], writes=pk("PA", 640, 768))
                    s.op("tensor", lambda E, c=c: E.matmul(
                        PA[0:65, 640:768], V1[:, c * 80: c * 80 + 65], ATS[:, :],
                        start=False, stop=True),
                        reads=["v1", "ats"], writes=pk("PA", 640, 768))
                    s.op("tensor", lambda E, c=c: E.matmul(
                        PA[:, 768:833], KT[:, c * 128:(c + 1) * 128], V1[:, c * 80: c * 80 + 65],
                        start=True, stop=True),
                        reads=[("kt", c), "v1"], writes=pk("PA", 768, 1024))
                    s.op("scalar", lambda E: E.copy(OYS[:, :], PA[0:64, 640:768]),
                         reads=pk("PA", 640, 768), writes=["oys"])
                    s.op("vector", lambda E: E.tensor_scalar_add(DENS[:, :], PA[64:65, 640:768], 1e-6),
                         reads=pk("PA", 640, 768) + ["oys"], writes=["dens"])
                    s.op("vector", lambda E: E.reciprocal(DENS[:, :], DENS[:, :]),
                         reads=["dens"], writes=["dens"])
                    s.op("tensor", lambda E: E.matmul(PA[0:64, 1024:1152], ONESR[0:1, 0:64],
                                                      DENS[:, :], start=True, stop=True),
                         reads=["dens", "onesr"], writes=pk("PA", 1024, 1152))
                    osl = OATT[(hd % 2) * 64:(hd % 2) * 64 + 64,
                               (hd // 2) * TPC + c * 128:(hd // 2) * TPC + (c + 1) * 128]
                    s.op("vector", lambda E, osl=osl: E.tensor_mul(
                        osl, OYS[:, :], PA[0:64, 1024:1152]),
                        reads=["oys"] + pk("PA", 1024, 1152), writes=[("oatt", hd // 2)])
                    s.op("vector", lambda E, hd=hd: E.tensor_add(
                        SST[:, hd * 65:(hd + 1) * 65], SST[:, hd * 65:(hd + 1) * 65],
                        PA[:, 768:833]),
                        reads=[("sst", hd), "sbf", "oys"] + pk("PA", 768, 1024),
                        writes=[("sst", hd)])

        RBROW = sb("RBROW", [1, 64], F32)
        s.dma("sync", RBROW[:, :], ap["rbrow"][:, :], writes=["rbrow"], stream="rbr")
        s.op("tensor", lambda E: E.matmul(PA[:, 0:64], ONESR[:, :], RBROW[:, :],
                                          start=True, stop=True),
             reads=["rbrow", "onesr"], writes=pk("PA", 0, 64))
        s.op("vector", lambda E: E.tensor_copy(RBB[:, :], PA[:, 0:64]),
             reads=pk("PA", 0, 64), writes=["rbb"])
        for c in range(16):
            for dc in range(8):
                s.op("tensor", lambda E, c=c, dc=dc: E.matmul(
                    PA[:, 64:128], XN[:, dc * TPC + c * 128: dc * TPC + (c + 1) * 128],
                    RW[:, dc * 64:(dc + 1) * 64], start=(dc == 0), stop=(dc == 7)),
                    reads=[("XN", dc), "rw"], writes=pk("PA", 64, 128))
            s.op("vector", lambda E, c=c: E.tensor_add(LGT[:, :], PA[:, 64:128], RBB[:, :]),
                 reads=pk("PA", 64, 128) + ["rbb"], writes=["lgt"])
            s.op("vector", lambda E: E.reduce_max(COLS[:, 0:1], LGT[:, :], axis=mybir.AxisListType.X,
                                                  negate=True),
                 reads=["lgt"], writes=["cols"])
            s.op("scalar", lambda E: E.activation(PRB[:, :], LGT[:, :], AF.Exp,
                                                  bias=COLS[:, 0:1], accum_out=COLS[:, 1:2]),
                 reads=["lgt", "cols"], writes=["prb", "cols2"])
            s.op("vector", lambda E: E.reciprocal(COLS[:, 2:3], COLS[:, 1:2]),
                 reads=["cols2"], writes=["cols3"])
            s.op("vector", lambda E: E.tensor_scalar_mul(PRB[:, :], PRB[:, :], COLS[:, 2:3]),
                 reads=["prb", "cols3"], writes=["prb"])
            s.op("vector", lambda E: E.tensor_copy(PMK[:, :], PRB[:, :]),
                 reads=["prb"], writes=["pmk"])
            s.op("vector", lambda E: E.memset(SELM[:, :], 0.0), writes=["selm"])
            s.op("vector", lambda E: E.memset(COLS[:, 3:4], 0.0), writes=["cols4"])
            for it in range(8):
                s.op("vector", lambda E: E.reduce_max(COLS[:, 4:5], PMK[:, :],
                                                      axis=mybir.AxisListType.X),
                     reads=["pmk"], writes=["cols5"])
                s.op("vector", lambda E: E.tensor_single_scalar(
                    LGT[:, :], PMK[:, :], COLS[:, 4:5], ALU.is_equal),
                    reads=["pmk", "cols5"], writes=["lgt"])
                s.op("vector", lambda E: E.tensor_add(SELM[:, :], SELM[:, :], LGT[:, :]),
                     reads=["selm", "lgt"], writes=["selm"])
                s.op("vector", lambda E: E.tensor_add(COLS[:, 3:4], COLS[:, 3:4], COLS[:, 4:5]),
                     reads=["cols4", "cols5"], writes=["cols4"])
                s.op("vector", lambda E: E.tensor_scalar_mul(COLS[:, 5:6], COLS[:, 4:5], -1.0),
                     reads=["cols5"], writes=["cols6"])
                s.op("vector", lambda E: E.scalar_tensor_tensor(
                    PMK[:, :], LGT[:, :], COLS[:, 5:6], PMK[:, :], ALU.mult, ALU.add),
                    reads=["lgt", "cols6", "pmk"], writes=["pmk"])
            s.op("vector", lambda E: E.reciprocal(COLS[:, 6:7], COLS[:, 3:4]),
                 reads=["cols4"], writes=["cols7"])
            s.op("vector", lambda E: E.tensor_scalar_mul(PRB[:, :], PRB[:, :], COLS[:, 6:7]),
                 reads=["prb", "cols7"], writes=["prb"])
            s.op("vector", lambda E: E.tensor_mul(CW[:, :], PRB[:, :], SELM[:, :]),
                 reads=["prb", "selm"], writes=["cw"])
            s.op("tensor", lambda E, c=c: E.transpose(PB16[0:64, 0:128], CW[:, :], IDENT[:, :]),
                 reads=["cw", "ident"], writes=["pb16"])
            s.op("vector", lambda E, c=c: E.tensor_copy(CWF[:, c * 128:(c + 1) * 128],
                                                        PB16[0:64, 0:128]),
                 reads=["pb16"], writes=["cwf"])

        NTT = 8
        for tt in range(NTT):
            t0 = tt * 256
            tc0 = tt * 2
            for e in range(64):
                wb = W1S[e % 2]
                s.dma("gpsimd", wb[:, :].rearrange("p (c o) -> p c o", c=8),
                      ap["w1"][e].rearrange("(c p) o -> p c o", p=128),
                      writes=[("W1S", e % 2)], stream=f"w1s{e % 2}")
                s.dma("sync", SELB[e % 2][:, :], ap["sels"][:, e * 128:(e + 1) * 128],
                      writes=[("selb", e % 2)], stream=f"sel{e % 2}")
                s.op("tensor", lambda E, e=e, tc0=tc0: E.matmul(
                    PA[:, 256:512], SELB[e % 2][:, :],
                    CWF[:, tc0 * 128: tc0 * 128 + 256], start=True, stop=True),
                    reads=[("selb", e % 2), "cwf"], writes=pk("PA", 256, 512))
                for hc in range(2):
                    pcol = 1024 + hc * 256
                    for dc in range(8):
                        s.op("tensor", lambda E, hc=hc, dc=dc, pcol=pcol, wb=wb, t0=t0: E.matmul(
                            PM[:, pcol:pcol + 256], wb[:, dc * 256 + hc * 128: dc * 256 + hc * 128 + 128],
                            XN[:, dc * TPC + t0: dc * TPC + t0 + 256],
                            start=(dc == 0), stop=(dc == 7)),
                            reads=[("W1S", e % 2), ("XN", dc)],
                            writes=pk("PM", pcol, pcol + 256))
                for hc in range(2):
                    ec = e * 2 + hc
                    pcol = 1024 + hc * 256
                    s.op("scalar", lambda E, ec=ec, hc=hc, e=e, pcol=pcol: E.activation(
                        SLAB[:, ec * 256:(ec + 1) * 256], PM[:, pcol:pcol + 256], AF.Silu,
                        bias=B1C[:, hc * 64 + e: hc * 64 + e + 1]),
                        reads=pk("PM", pcol, pcol + 256) + ["b1c"], writes=[("slab", ec)])
                    s.op("vector", lambda E, ec=ec: E.tensor_mul(
                        SLAB[:, ec * 256:(ec + 1) * 256], SLAB[:, ec * 256:(ec + 1) * 256],
                        PA[:, 256:512]),
                        reads=[("slab", ec)] + pk("PA", 256, 512), writes=[("slab", ec)])
            grp = lambda c2, dh: (c2 * 2 + dh) * 512
            for c2 in range(2):
                for dh in range(2):
                    s.op("tensor", lambda E, c2=c2, dh=dh: E.matmul(
                        PM[:, grp(c2, dh):grp(c2, dh) + 512], ONESRB[:, :],
                        BOS[0:1, dh * 512:(dh + 1) * 512], start=True, stop=False),
                        reads=["bos", "onesr"], writes=pk("PM", grp(c2, dh), grp(c2, dh) + 512))
                    s.op("tensor", lambda E, c2=c2, dh=dh: E.matmul(
                        PM[:, grp(c2, dh):grp(c2, dh) + 512], ONESRB[:, :],
                        SB2A[0:1, dh * 512:(dh + 1) * 512], start=False, stop=False),
                        reads=["sb2a", "onesr"], writes=pk("PM", grp(c2, dh), grp(c2, dh) + 512))
                    s.op("tensor", lambda E, c2=c2, dh=dh: E.matmul(
                        PM[:, grp(c2, dh):grp(c2, dh) + 512], ONESRB[:, :],
                        SB2B[0:1, dh * 512:(dh + 1) * 512], start=False, stop=False),
                        reads=["sb2b", "onesr"], writes=pk("PM", grp(c2, dh), grp(c2, dh) + 512))
                    s.op("tensor", lambda E, c2=c2, dh=dh, tc0=tc0: E.matmul(
                        PM[:, grp(c2, dh):grp(c2, dh) + 512],
                        CWF[:, (tc0 + c2) * 128:(tc0 + c2 + 1) * 128],
                        B2S[:, dh * 512:(dh + 1) * 512], start=False, stop=False),
                        reads=["cwf", "b2s"], writes=pk("PM", grp(c2, dh), grp(c2, dh) + 512))
            for dc in range(8):
                wo = WOTS[dc % 2]
                s.dma("gpsimd", wo[:, :], ap["woT"][dc * 128:(dc + 1) * 128, :],
                      writes=[("W2S", dc % 2)], stream=f"w2s{dc % 2}")
                for c2 in range(2):
                    for dh in range(2):
                        s.op("tensor", lambda E, dc=dc, c2=c2, dh=dh, wo=wo, tc0=tc0: E.matmul(
                            PM[:, grp(c2, dh):grp(c2, dh) + 512],
                            OATT[:, dc * TPC + (tc0 + c2) * 128: dc * TPC + (tc0 + c2 + 1) * 128],
                            wo[:, dh * 512:(dh + 1) * 512], start=False, stop=False),
                            reads=[("W2S", dc % 2), ("oatt", dc)],
                            writes=pk("PM", grp(c2, dh), grp(c2, dh) + 512))
            for se in range(2):
                for hcg in range(16):
                    wb = W1S[hcg % 2]
                    s.dma("gpsimd", wb[:, 0:1024].rearrange("p (c o) -> p c o", c=8),
                          ap["sw1"][se][:, hcg * 128:(hcg + 1) * 128]
                          .rearrange("(c p) o -> p c o", p=128),
                          writes=[("W1S", hcg % 2)], stream=f"w1s{hcg % 2}")
                    pcol = (hcg % 2) * 256
                    for dc in range(8):
                        s.op("tensor", lambda E, dc=dc, pcol=pcol, wb=wb, t0=t0: E.matmul(
                            PA[:, pcol:pcol + 256], wb[:, dc * 128:(dc + 1) * 128],
                            XN[:, dc * TPC + t0: dc * TPC + t0 + 256],
                            start=(dc == 0), stop=(dc == 7)),
                            reads=[("W1S", hcg % 2), ("XN", dc)],
                            writes=pk("PA", pcol, pcol + 256))
                    s.op("scalar", lambda E, hcg=hcg, se=se, pcol=pcol: E.activation(
                        SLAB2[:, hcg * 256:(hcg + 1) * 256], PA[:, pcol:pcol + 256], AF.Silu,
                        bias=SB1C[:, hcg * 2 + se: hcg * 2 + se + 1]),
                        reads=pk("PA", pcol, pcol + 256) + ["sb1c"], writes=[("slab2", hcg)])
                for hcg in range(16):
                    wb = W2S[hcg % 2]
                    s.dma("gpsimd", wb[:, :], ap["sw2r"][(se * 16 + hcg) * 128:(se * 16 + hcg + 1) * 128, :],
                          writes=[("W2S", hcg % 2)], stream=f"w2s{hcg % 2}")
                    for c2 in range(2):
                        for dh in range(2):
                            s.op("tensor", lambda E, hcg=hcg, c2=c2, dh=dh, wb=wb: E.matmul(
                                PM[:, grp(c2, dh):grp(c2, dh) + 512],
                                SLAB2[:, hcg * 256 + c2 * 128: hcg * 256 + c2 * 128 + 128],
                                wb[:, dh * 512:(dh + 1) * 512], start=False, stop=False),
                                reads=[("W2S", hcg % 2), ("slab2", hcg)],
                                writes=pk("PM", grp(c2, dh), grp(c2, dh) + 512))
            for ec in range(128):
                wb = W2S[ec % 2]
                s.dma("gpsimd", wb[:, :], ap["w2r"][ec * 128:(ec + 1) * 128, :],
                      writes=[("W2S", ec % 2)], stream=f"w2s{ec % 2}")
                for c2 in range(2):
                    for dh in range(2):
                        last = (ec == 127)
                        s.op("tensor", lambda E, ec=ec, c2=c2, dh=dh, wb=wb, last=last: E.matmul(
                            PM[:, grp(c2, dh):grp(c2, dh) + 512],
                            SLAB[:, ec * 256 + c2 * 128: ec * 256 + c2 * 128 + 128],
                            wb[:, dh * 512:(dh + 1) * 512], start=False, stop=(last and dh >= 0)),
                            reads=[("W2S", ec % 2), ("slab", ec)],
                            writes=pk("PM", grp(c2, dh), grp(c2, dh) + 512))
            for c2 in range(2):
                for dh in range(2):
                    s.dma("sync", X1S[:, :],
                          ap["x1tm"][(tc0 + c2) * 128:(tc0 + c2 + 1) * 128,
                                     dh * 512:(dh + 1) * 512],
                          writes=["x1s"], stream="x1")
                    s.op("vector", lambda E, c2=c2, dh=dh: E.tensor_add(
                        OUT[:, :], PM[:, grp(c2, dh):grp(c2, dh) + 512], X1S[:, :]),
                        reads=pk("PM", grp(c2, dh), grp(c2, dh) + 512) + ["x1s"],
                        writes=["out"])
                    s.dma("sync", out_o[(tc0 + c2) * 128:(tc0 + c2 + 1) * 128,
                                        dh * 512:(dh + 1) * 512], OUT[:, :],
                          reads=["out"], stream="out")

        s.emit(es.enter_context(nc.Block()))
    return nc


def prep_launch_D(inputs, xn1fm_cores, x1_cores, kvt_cores):
    trj = np.triu(np.ones((128, 128))).astype(BF16NP)
    ident = np.eye(128).astype(BF16NP)
    sels = np.zeros((64, 64, 128), BF16NP)
    for e in range(64):
        sels[e, e, :] = 1.0
    sels = sels.transpose(1, 0, 2).reshape(64, 64 * 128)
    w1 = np.asarray(inputs["L1_ffn_w1"], np.float32)
    b1 = np.asarray(inputs["L1_ffn_b1"], np.float32)
    w2 = np.asarray(inputs["L1_ffn_w2"], np.float32)
    sw1 = np.asarray(inputs["L1_ffn_sw1"], np.float32)
    sb1 = np.asarray(inputs["L1_ffn_sb1"], np.float32)
    sw2 = np.asarray(inputs["L1_ffn_sw2"], np.float32)
    b1r = b1.reshape(64, 2, 128).transpose(2, 1, 0).copy()
    sb1r = sb1.reshape(2, 16, 128).transpose(2, 1, 0).copy()
    common = dict(
        kvpz=None,
        wqT=np.ascontiguousarray(np.asarray(inputs["L1_attn_Wq"], np.float32).T),
        wkT=np.ascontiguousarray(np.asarray(inputs["L1_attn_Wk"], np.float32).T),
        wvT=np.ascontiguousarray(np.asarray(inputs["L1_attn_Wv"], np.float32).T),
        woT=np.ascontiguousarray(np.asarray(inputs["L1_attn_Wo"], np.float32).T),
        bqr=np.asarray(inputs["L1_attn_bq"], np.float32).reshape(8, 128).T.copy(),
        bkr=np.asarray(inputs["L1_attn_bk"], np.float32).reshape(8, 128).T.copy(),
        bvr=np.asarray(inputs["L1_attn_bv"], np.float32).reshape(8, 128).T.copy(),
        borow=np.asarray(inputs["L1_attn_bo"], np.float32).reshape(1, D),
        projT=np.ascontiguousarray(np.asarray(inputs["L1_attn_proj"], np.float32).T),
        rw=np.asarray(inputs["L1_ffn_router_w"], np.float32),
        rbrow=np.asarray(inputs["L1_ffn_router_b"], np.float32).reshape(1, 64),
        w1=w1, b1r=b1r, w2r=w2.reshape(64 * 256, D),
        b2=np.asarray(inputs["L1_ffn_b2"], np.float32),
        sw1=sw1, sb1r=sb1r, sw2r=sw2.reshape(2 * 2048, D),
        sb2=np.asarray(inputs["L1_ffn_sb2"], np.float32),
        sels=sels, ident=ident, triu=trj,
    )
    del common["kvpz"]
    maps = []
    for c in range(NCORES):
        kvp = np.zeros((3, 16, 128, 65), np.float32)
        npred = c % 4
        for i in range(npred):
            kvp[i] = kvt_cores[(c // 4) * 4 + i]
        m = dict(common)
        m.update(xn1fm=xn1fm_cores[c],
                 x1tm=np.ascontiguousarray(x1_cores[c].T),
                 kvp=kvp)
        maps.append(m)
    return maps


def prep_launch_C(inputs, oa_full, of_full, xl5f_cores):
    x = np.asarray(inputs["L0_attn_x"], np.float32)
    common = dict(
        wo=np.asarray(inputs["L0_attn_wo"], np.float32),
        fwo=np.asarray(inputs["L0_ffn_Wo"], np.float32),
        fwff=np.asarray(inputs["L0_ffn_Wff"], np.float32),
        alng=np.asarray(inputs["L0_attn_attn_ln_g"], np.float32).reshape(8, 128).T.copy(),
        alnb=np.asarray(inputs["L0_attn_attn_ln_b"], np.float32).reshape(8, 128).T.copy(),
        wkT=np.ascontiguousarray(np.asarray(inputs["L1_attn_Wk"], np.float32).T),
        wvT=np.ascontiguousarray(np.asarray(inputs["L1_attn_Wv"], np.float32).T),
        bkr=np.asarray(inputs["L1_attn_bk"], np.float32).reshape(8, 128).T.copy(),
        bvr=np.asarray(inputs["L1_attn_bv"], np.float32).reshape(8, 128).T.copy(),
        projT=np.ascontiguousarray(np.asarray(inputs["L1_attn_proj"], np.float32).T),
    )
    maps = []
    for c in range(NCORES):
        b, r0 = c // 4, (c % 4) * TPC
        m = dict(common)
        m.update(oafm=np.ascontiguousarray(oa_full[b][:, r0:r0 + TPC]),
                 offm=np.ascontiguousarray(of_full[b][:, r0:r0 + TPC]),
                 xl5f=xl5f_cores[c],
                 xfm0=np.ascontiguousarray(x[b, r0:r0 + TPC].T))
        maps.append(m)
    return maps


def kernel(**inputs):
    ncA = build_launch_A()
    resA = run_bass_kernel_spmd(ncA, prep_launch_A(inputs),
                                core_ids=list(range(NCORES))).results
    xl5fm_full = [np.concatenate([resA[b * 4 + i]["xl5b"] for i in range(4)], axis=1)
                  for b in range(2)]
    xn0fm_full = [np.concatenate([resA[b * 4 + i]["xn0b"] for i in range(4)], axis=1)
                  for b in range(2)]

    ncB = build_launch_B()
    resB = run_bass_kernel_spmd(ncB, prep_launch_B(inputs, xl5fm_full, xn0fm_full),
                                core_ids=list(range(NCORES))).results
    oa_full = [np.concatenate([resB[b * 4 + i]["attn_o"].reshape(256, T) for i in range(4)],
                              axis=0) for b in range(2)]
    of_full = [np.concatenate([resB[b * 4 + i]["ffn_o"].reshape(256, T) for i in range(4)],
                              axis=0) for b in range(2)]

    ncC = build_launch_C()
    resC = run_bass_kernel_spmd(ncC, prep_launch_C(inputs, oa_full, of_full,
                                                   [resA[c]["xl5f"] for c in range(NCORES)]),
                                core_ids=list(range(NCORES))).results

    ncD = build_launch_D()
    resD = run_bass_kernel_spmd(
        ncD, prep_launch_D(inputs, [resC[c]["xn1b"] for c in range(NCORES)],
                           [resC[c]["x1f"] for c in range(NCORES)],
                           [resC[c]["kvt"] for c in range(NCORES)]),
        core_ids=list(range(NCORES))).results

    out = np.zeros((B, T, D), np.float32)
    for c in range(NCORES):
        b, r0 = c // 4, (c % 4) * TPC
        out[b, r0:r0 + TPC] = resD[c]["outtm"]
    return out
